# version 33
# baseline (speedup 1.0000x reference)
import math
from contextlib import ExitStack

import numpy as np
import ml_dtypes
import concourse.bass as bass
import concourse.mybir as mybir
from concourse.bass_utils import run_bass_kernel_spmd

F32 = mybir.dt.float32
BF16 = mybir.dt.bfloat16
I32 = mybir.dt.int32
AF = mybir.ActivationFunctionType
ALU = mybir.AluOpType
AX = mybir.AxisListType.X

ENGS = ("tensor", "vector", "scalar", "gpsimd", "sync")
DH = 128
NEXP = 8
RANK = 16
ISCALE = float(1.0 / math.sqrt(DH))
LNEPS = 1e-5


class Rec:
    def __init__(self):
        self.items = []

    def __getattr__(self, name):
        if name.startswith("_"):
            raise AttributeError(name)
        items = self.items
        def call(*a, **k):
            rec = [name, a, k, None]
            items.append(rec)
            class _G:
                def then_inc(self, sem, n):
                    rec[3] = (sem, n)
                    return self
            return _G()
        return call

    def replay(self, eng):
        for name, a, k, inc in self.items:
            r = getattr(eng, name)(*a, **k)
            if inc is not None:
                r.then_inc(*inc)


class Cfg:
    def __init__(self, T=4096, E=2048, NH=16, DL=512, NG=4, INTER=5504):
        assert E == NH * DH
        self.T, self.E, self.NH, self.DL, self.NG, self.INTER = T, E, NH, DL, NG, INTER
        self.NT = T // 128
        self.TB = T // 512
        self.NE = E // 128
        self.NL = DL // 128
        self.NI = INTER // 128
        assert T % 512 == 0 and DL % 128 == 0 and INTER % 128 == 0


class Builder:
    def __init__(self, cfg: Cfg, debug_outputs=()):
        self.cfg = cfg
        self.dbg = set(debug_outputs)
        self.nc = bass.Bass("TRN2", target_bir_lowering=False)
        self.q = {e: Rec() for e in ENGS}
        self.stack = ExitStack()
        self.sems = {}
        self.val = {}

    @staticmethod
    def _canon(name):
        import re as _re
        return _re.sub(r"^p\d+b?_", "", name)

    def sem(self, name):
        name = self._canon(name)
        if name not in self.sems:
            self.sems[name] = self.stack.enter_context(self.nc.semaphore(name))
            self.val[name] = 0
        return self.sems[name]

    def bump(self, name, n=1):
        s = self.sem(name)
        name = self._canon(name)
        self.val[name] += n
        return s, self.val[name]

    def cur(self, name):
        self.sem(name)
        return self.val[self._canon(name)]

    def phase_done(self, ph):
        return [(n, self.val[n]) for n in list(self.val)
                if n.startswith("st") or n.startswith("sv")]

    def fence(self, rec, done):
        for (n, v) in done:
            rec.wait_ge(self.sem(n), v)

    def dma_entry_fence(self):
        for ename in ("gpsimd", "sync", "scalar"):
            rec = self.q[ename]
            for n, v in list(self.val.items()):
                if v:
                    rec.wait_ge(self.sems[n], v)

    def dram(self, name, shape, dtype, kind="Internal"):
        if name in self.dbg and kind == "Internal":
            kind = "ExternalOutput"
        return self.nc.dram_tensor(name, list(shape), dtype, kind=kind).ap()

    def sb(self, name, shape, dtype, **kw):
        return self.nc.alloc_sbuf_tensor(name, list(shape), dtype, **kw).ap()

    def sbt(self, st, name, shape, dtype, **kw):
        return st.enter_context(self.nc.sbuf_tensor(name, list(shape), dtype, **kw)).ap()

    def pst(self, st, name, shape, dtype=F32):
        return st.enter_context(self.nc.psum_tensor(name, list(shape), dtype)).ap()

    def declare_io(self):
        c, d = self.cfg, self.dram
        self.x = d("x", [c.T, c.E], F32, "ExternalInput")
        self.wqT = d("wqT", [c.E, c.E], F32, "ExternalInput")
        self.wdkvT = d("wdkvT", [c.E, c.DL], F32, "ExternalInput")
        self.wupkT = d("wupkT", [c.NH, c.DL, DH], F32, "ExternalInput")
        self.wupvT = d("wupvT", [c.NH, c.DL, DH], F32, "ExternalInput")
        self.woT = d("woT", [c.E, c.E], F32, "ExternalInput")
        self.wffT = d("wffT", [c.E, c.E], F32, "ExternalInput")
        self.gatewT = d("gatewT", [c.E, NEXP], F32, "ExternalInput")
        self.awT = d("awT", [c.E, 128], F32, "ExternalInput")
        self.bwT = d("bwT", [128, c.E], F32, "ExternalInput")
        self.wq1 = d("wq1", [c.NH, c.E, DH], F32, "ExternalInput")
        self.bq1 = d("bq1", [c.NH, DH], F32, "ExternalInput")
        self.wk1 = d("wk1", [c.NG, c.E, DH], F32, "ExternalInput")
        self.bk1 = d("bk1", [c.NG, DH], F32, "ExternalInput")
        self.wv1 = d("wv1", [c.NG, c.E, DH], F32, "ExternalInput")
        self.bv1 = d("bv1", [c.NG, DH], F32, "ExternalInput")
        self.lut = d("lut", [1, 16], F32, "ExternalInput")
        self.gidx = d("gidx", [c.INTER, c.E], I32, "ExternalInput")
        self.uidx = d("uidx", [c.INTER, c.E], I32, "ExternalInput")
        self.didx = d("didx", [c.E, c.INTER], I32, "ExternalInput")

        self.OUT_T = d("OUT_T", [c.E, c.T], F32, "ExternalOutput")

        self.xb = d("xb", [c.T, c.E], BF16)
        self.QT0 = d("QT0", [c.E, c.T], BF16)
        self.CT0 = d("CT0", [c.DL, c.T], BF16)
        self.OT0 = d("OT0", [c.E, c.T], BF16)
        self.X1 = d("X1", [c.E, c.T], BF16)
        self.X2 = d("X2", [c.E, c.T], BF16)
        self.X3 = d("X3", [c.E, c.T], BF16)
        self.QT1 = d("QT1", [c.E, c.T], BF16)
        self.KT1 = d("KT1", [c.NG * DH, c.T], BF16)
        self.V1 = d("V1", [c.NG, c.T, DH], BF16)
        self.GWB = d("GWB", [c.INTER, c.E], BF16)
        self.UWB = d("UWB", [c.INTER, c.E], BF16)
        self.DWB = d("DWB", [c.E, c.INTER], BF16)
        self.GWT = d("GWT", [c.E, c.INTER], BF16)
        self.UWT = d("UWT", [c.E, c.INTER], BF16)
        self.DWT = d("DWT", [c.INTER, c.E], BF16)

    def declare_consts(self):
        nc = self.nc
        c_ones = nc.inline_tensor(ml_dtypes.bfloat16(np.ones((128, 1), np.float32)), "c_ones").ap()
        c_onesr = nc.inline_tensor(np.ones((1, 128), np.float32), "c_onesr").ap()
        c_ident = nc.inline_tensor(ml_dtypes.bfloat16(np.eye(128, dtype=np.float32)), "c_ident").ap()
        e16 = np.zeros((NEXP, 128), np.float32)
        for e in range(NEXP):
            e16[e, e * RANK:(e + 1) * RANK] = 16.0
        c_e16 = nc.inline_tensor(ml_dtypes.bfloat16(e16), "c_e16").ap()
        masks = []
        for di in range(4):
            m = (np.arange(512)[None, :] >= (np.arange(128)[:, None] + di * 128)).astype(np.float32)
            masks.append(m)
        c_masks = nc.inline_tensor(ml_dtypes.bfloat16(np.concatenate(masks, 1)), "c_masks").ap()

        self.ones_sb = self.sb("ones_sb", [128, 1], BF16)
        self.onesr_sb = self.sb("onesr_sb", [1, 128], F32)
        self.id_sb = self.sb("id_sb", [128, 128], BF16)
        self.e16_sb = self.sb("e16_sb", [NEXP, 128], BF16)
        self.lutb_sb = self.sb("lutb_sb", [128, 16], F32)
        self.c_masks_dram = None
        self.lut_sb = self.sb("lut_sb", [1, 16], F32)

        g = self.q["gpsimd"]
        self.c_masks = c_masks
        s5, v5 = self.bump("ld_const", 5 * 16)
        g.dma_start(self.ones_sb, c_ones).then_inc(s5, 16)
        g.dma_start(self.onesr_sb, c_onesr).then_inc(s5, 16)
        g.dma_start(self.id_sb, c_ident).then_inc(s5, 16)
        g.dma_start(self.e16_sb, c_e16).then_inc(s5, 16)
        g.dma_start(self.lut_sb, self.lut).then_inc(s5, 16)
        self.const_ready = ("ld_const", v5)

    def make_ln_res(self, st, ph, sum_ps, sq_ps, mb_ps, rb_ps, out_dtype, depth=1):
        r = {
            "z_sb": [self.sbt(st, f"{ph}_z{i}", [128, self.cfg.NE * 512], F32)
                     for i in range(depth)],
            "zb": [self.sbt(st, f"{ph}_zb{i}", [128, 512], BF16) for i in range(2)],
            "zq": [self.sbt(st, f"{ph}_zq{i}", [128, 512], BF16) for i in range(2)],
            "outb": [self.sbt(st, f"{ph}_ob{i}", [128, 512], out_dtype) for i in range(2)],
            "t_sb": self.sbt(st, f"{ph}_t", [128, 512], F32),
            "rows": self.sbt(st, f"{ph}_rows", [128, 2048], F32),
            "mb_sb": [self.sbt(st, f"{ph}_mb{i}", [128, 512], F32) for i in range(depth)],
            "rb_sb": [self.sbt(st, f"{ph}_rb{i}", [128, 512], F32) for i in range(depth)],
            "sum_ps": sum_ps, "sq_ps": sq_ps, "mb_ps": mb_ps, "rb_ps": rb_ps,
            "stats_mm": [0, 0], "out_st": [0, 0], "depth": depth,
        }
        rows = r["rows"]
        r["m_sb"] = [rows[0:1, 0:512], rows[0:1, 512:1024]][:max(depth, 1)]
        r["r_sb"] = [rows[0:1, 1024:1536], rows[0:1, 1536:2048]][:max(depth, 1)]
        r["a_sb"] = rows[32:33, 0:512]
        r["b_sb"] = rows[32:33, 512:1024]
        r["c_sb"] = rows[64:65, 0:512]
        r["d_sb"] = rows[64:65, 512:1024]
        return r

    def _ln_emit_stats(self, ph, r):
        t = self.q["tensor"]
        NE = self.cfg.NE
        pend = r.get("pend_stats")
        if pend is None:
            return
        r["pend_stats"] = None
        j, szq, vzq = pend
        t.wait_ge(szq, vzq)
        t.matmul(r["sum_ps"], self.ones_sb, r["zb"][j % 2],
                 start=(j == 0), stop=(j == NE - 1), skip_group_check=True)
        mi = t.matmul(r["sq_ps"], self.ones_sb, r["zq"][j % 2],
                      start=(j == 0), stop=(j == NE - 1), skip_group_check=True)
        smm, vmm = self.bump(f"{ph}_mm")
        mi.then_inc(smm, 1)
        r["stats_mm"][j % 2] = vmm

    def ln_z(self, ph, r, j, ps_ap, ps_waits, res_ap, bank=None):
        cfg = self.cfg
        t, v = self.q["tensor"], self.q["vector"]
        NE = cfg.NE
        if bank is None:
            bank = j % 2
        self._ln_emit_stats(ph, r)
        for (sw, vw) in ps_waits:
            v.wait_ge(sw, vw)
        if r["stats_mm"][j % 2]:
            v.wait_ge(self.sem(f"{ph}_mm"), r["stats_mm"][j % 2])
        sc = self.q["scalar"]
        zp = r.get("cur_par", 0)
        zt = r["z_sb"][zp][:, j * 512:(j + 1) * 512]
        sz, vz = self.bump(f"{ph}_zadd{bank}")
        v.tensor_tensor(zt, ps_ap, res_ap, ALU.add).then_inc(sz, 1)
        sc.wait_ge(sz, vz)
        if r["stats_mm"][j % 2]:
            sc.wait_ge(self.sem(f"{ph}_mm"), r["stats_mm"][j % 2])
        szb, vzb = self.bump(f"{ph}_vz")
        sc.copy(r["zb"][j % 2], zt).then_inc(szb, 1)
        szq, vzq = self.bump(f"{ph}_vz")
        sc.activation(r["zq"][j % 2], zt, AF.Square).then_inc(szq, 1)
        r["pend_stats"] = (j, szq, vzq)

    def ln_stats(self, ph, r, tb):
        self._ln_emit_stats(ph, r)
        cfg = self.cfg
        t, v, sc = self.q["tensor"], self.q["vector"], self.q["scalar"]
        zp = r.get("cur_par", 0)
        smm, vmm = self.sem(f"{ph}_mm"), self.cur(f"{ph}_mm")
        v.wait_ge(smm, vmm)
        if r.get("mbmm"):
            v.wait_ge(self.sem(f"{ph}_mm"), r["mbmm"])
        s1, _ = self.bump(f"{ph}_vc")
        v.tensor_scalar(r["m_sb"][zp], r["sum_ps"], 1.0 / cfg.E, None, ALU.mult).then_inc(s1, 1)
        s2, v2 = self.bump(f"{ph}_vc")
        v.tensor_scalar(r["a_sb"], r["sq_ps"], 1.0 / cfg.E, None, ALU.mult).then_inc(s2, 1)
        v.wait_ge(s2, v2)
        s3, v3 = self.bump(f"{ph}_vc")
        v.tensor_tensor(r["b_sb"], r["m_sb"][zp], r["m_sb"][zp], ALU.mult).then_inc(s3, 1)
        v.wait_ge(s3, v3)
        s4, v4 = self.bump(f"{ph}_vc")
        v.scalar_tensor_tensor(r["c_sb"], r["a_sb"], LNEPS, r["b_sb"],
                               ALU.add, ALU.subtract).then_inc(s4, 1)
        sc.wait_ge(s4, v4)
        s5, v5 = self.bump(f"{ph}_ac")
        sc.activation(r["d_sb"], r["c_sb"], AF.Ln).then_inc(s5, 1)
        sc.wait_ge(s5, v5)
        s6, v6 = self.bump(f"{ph}_ac")
        sc.activation(r["r_sb"][zp], r["d_sb"], AF.Exp, scale=-0.5).then_inc(s6, 1)
        r.setdefault("s6_by", {})[zp] = (s6, v6)

    def ln_norm(self, ph, r, tb, dest):
        cfg = self.cfg
        t, v, s = self.q["tensor"], self.q["vector"], self.q["sync"]
        NE = cfg.NE
        zp = r.get("norm_par", 0)
        t.wait_ge(*r["s6_by"][zp])
        if r.get("free_wait"):
            t.wait_ge(*r["free_wait"])
        for w_ in r.get("extra_norm_wait", ()):
            t.wait_ge(*w_)
        mi = t.matmul(r["mb_ps"], self.onesr_sb, r["m_sb"][zp], start=True, stop=True)
        mi2 = t.matmul(r["rb_ps"], self.onesr_sb, r["r_sb"][zp], start=True, stop=True)
        smm2, vmm2 = self.bump(f"{ph}_mm")
        mi2.then_inc(smm2, 1)
        r["mbmm"] = vmm2
        v.wait_ge(smm2, vmm2)
        s7, _ = self.bump(f"{ph}_vc")
        v.tensor_copy(r["mb_sb"][zp], r["mb_ps"]).then_inc(s7, 1)
        s8, v8 = self.bump(f"{ph}_vc")
        v.tensor_copy(r["rb_sb"][zp], r["rb_ps"]).then_inc(s8, 1)
        v.wait_ge(s8, v8)
        r["free_wait"] = (self.sem(f"{ph}_vc"), v8)
        for j in range(NE):
            zt = r["z_sb"][zp][:, j * 512:(j + 1) * 512]
            ob = r["outb"][j % 2]
            if r["out_st"][j % 2]:
                v.wait_ge(self.sem(f"{ph}_st{j % 2}"), r["out_st"][j % 2])
            if j > 0:
                v.wait_ge(self.sem(f"{ph}_vc"), self.cur(f"{ph}_vc"))
            s9, v9 = self.bump(f"{ph}_vc")
            v.tensor_tensor(r["t_sb"], zt, r["mb_sb"][zp], ALU.subtract).then_inc(s9, 1)
            v.wait_ge(s9, v9)
            s10, v10 = self.bump(f"{ph}_vc")
            v.tensor_tensor(ob, r["t_sb"], r["rb_sb"][zp], ALU.mult).then_inc(s10, 1)
            s.wait_ge(s10, v10)
            if r["out_st"][j % 2]:
                s.wait_ge(self.sem(f"{ph}_st{j % 2}"), r["out_st"][j % 2])
            sst, vst = self.bump(f"{ph}_st{j % 2}", 16)
            s.dma_start(dest[j * 128:(j + 1) * 128, tb * 512:(tb + 1) * 512], ob).then_inc(sst, 16)
            r["out_st"][j % 2] = vst

    def ln_finish(self, ph, r, tb, dest):
        r["norm_par"] = r.get("cur_par", 0)
        self.ln_stats(ph, r, tb)
        self.ln_norm(ph, r, tb, dest)

    def p0(self):
        c = self.cfg
        g, s = self.q["gpsimd"], self.q["sync"]
        st = ExitStack()
        buf = [self.sbt(st, f"p0_buf{i}", [128, c.E], BF16) for i in range(4)]
        n = c.T // 128
        for i in range(n):
            p = i % 4
            if i >= 4:
                g.wait_ge(self.sem(f"p0_st{p}"), 16 * (i // 4))
            if self.cur(f"p0_ld{p}"):
                g.wait_ge(self.sem(f"p0_ld{p}"), self.cur(f"p0_ld{p}"))
            sl, vl = self.bump(f"p0_ld{p}", 16)
            g.dma_start(buf[p], self.x[i * 128:(i + 1) * 128, :]).then_inc(sl, 16)
            s.wait_ge(sl, vl)
            ss, _ = self.bump(f"p0_st{p}", 16)
            s.dma_start(self.xb[i * 128:(i + 1) * 128, :], buf[p]).then_inc(ss, 16)
        self.p0_done = self.phase_done("p0")
        self.last_done = self.p0_done
        st.close()

    def build_init(self):
        c = self.cfg
        t, v, g = self.q["tensor"], self.q["vector"], self.q["gpsimd"]
        self.bw_stack = ExitStack()
        self.bw_idx = [self.sbt(self.bw_stack, "bw_idx0", [128, 1024], BF16)]
        self.bw_acc = self.sbt(self.bw_stack, "bw_acc", [128, 1024], BF16)
        self.bw_tmp = self.sbt(self.bw_stack, "bw_tmp", [128, 1024], BF16)
        self.bw_jobs = []
        for (srct, dstt, nrow, w) in ((self.gidx, self.GWB, c.INTER, c.E),
                                      (self.uidx, self.UWB, c.INTER, c.E),
                                      (self.didx, self.DWB, c.E, c.INTER)):
            r0 = 0
            while r0 < nrow:
                rh = min(128, nrow - r0)
                for c0 in range(0, w, 1024):
                    cw = min(1024, w - c0)
                    self.bw_jobs.append((srct, dstt, r0, rh, c0, cw))
                r0 += rh
        self.bw_next = 0
        self.bw_cur = None
        self.bw_step = 0
        self.bw_dve = [0, 0]
        self.bw_sta = 0
        self.tr_stack = None
        self.tr_jobs = []
        bases = {id(self.GWB): (0, 2), id(self.UWB): (86, 2), id(self.DWB): (172, 6)}
        for (src, dstt, sr, sw) in ((self.GWB, self.GWT, c.INTER, c.E),
                                    (self.UWB, self.UWT, c.INTER, c.E),
                                    (self.DWB, self.DWT, c.E, c.INTER)):
            base, nch = bases[id(src)]
            chunks = []
            off = 0
            while off < sr:
                ck = min(1024, sr - off)
                chunks.append((off, ck))
                off += ck
            for band in range(sw // 128):
                c_idx = (band * 128) // 1024
                for (off, ck) in chunks:
                    last_rb = (off + ck) // 128 - 1
                    need = base + last_rb * nch + c_idx
                    self.tr_jobs.append(
                        (need,
                         src[off:off + ck, band * 128:(band + 1) * 128],
                         dstt[band * 128:(band + 1) * 128, off:off + ck], ck))
        self.tr_next = 0
        self.tr_st = [0, 0]
        self.tr_pend = None
        with self.nc.psum_tensor("bw_lb", [128, 16], F32) as lb_t:
            lb_ps = lb_t.ap()
            t.wait_ge(self.sem(self.const_ready[0]), self.const_ready[1])
            mi = t.matmul(lb_ps, self.onesr_sb, self.lut_sb, start=True, stop=True)
            smm, vmm = self.bump("bw_mm")
            mi.then_inc(smm, 1)
            v.wait_ge(smm, vmm)
            slb, vlb = self.bump("bw_vc")
            v.tensor_copy(self.lutb_sb, lb_ps).then_inc(slb, 1)
            self.bw_lut_ready = vlb

    def emit_build(self, n):
        self.emit_build_ops(31 * n)

    def emit_build_ops(self, nops):
        v, g, s = self.q["vector"], self.q["gpsimd"], self.q["sync"]
        while nops > 0:
            if self.bw_cur is None:
                if self.bw_next >= len(self.bw_jobs):
                    return
                ji = self.bw_next
                self.bw_next += 1
                srct, dstt, r0, rh, c0, cw = self.bw_jobs[ji]
                if self.bw_dve[0]:
                    g.wait_ge(self.sem("bw_vc"), self.bw_dve[0])
                if self.cur("bw_ldi0"):
                    g.wait_ge(self.sem("bw_ldi0"), self.cur("bw_ldi0"))
                sl, vl = self.bump("bw_ldi0", 16)
                g.dma_start(self.bw_idx[0][0:rh, 0:cw],
                            srct[r0:r0 + rh, c0:c0 + cw]).then_inc(sl, 16)
                self.bw_cur = (ji, dstt, r0, rh, c0, cw, sl, vl)
                self.bw_step = 0
            ji, dstt, r0, rh, c0, cw, sl, vl = self.bw_cur
            idx = self.bw_idx[0][0:rh, 0:cw]
            acc = self.bw_acc[0:rh, 0:cw]
            tmp = self.bw_tmp[0:rh, 0:cw]
            vv = self.bw_step
            if vv == 0:
                v.wait_ge(sl, vl)
                if ji == 0:
                    v.wait_ge(self.sem("bw_vc"), self.bw_lut_ready)
                if self.bw_sta:
                    v.wait_ge(self.sem("bw_sta"), self.bw_sta)
                s0, _ = self.bump("bw_vc")
                v.tensor_scalar(acc, idx, 0.0, self.lutb_sb[0:rh, 0:1],
                                ALU.is_equal, ALU.mult).then_inc(s0, 1)
                nops -= 1
            else:
                s1, v1 = self.bump("bw_vc")
                v.tensor_scalar(tmp, idx, float(vv), self.lutb_sb[0:rh, vv:vv + 1],
                                ALU.is_equal, ALU.mult).then_inc(s1, 1)
                v.wait_ge(s1, v1)
                s2, v2 = self.bump("bw_vc")
                v.tensor_tensor(acc, acc, tmp, ALU.add).then_inc(s2, 1)
                v.wait_ge(s2, v2)
                nops -= 2
            if vv == 15:
                self.bw_dve[0] = self.cur("bw_vc")
                s.wait_ge(self.sem("bw_vc"), self.cur("bw_vc"))
                if self.bw_sta:
                    s.wait_ge(self.sem("bw_sta"), self.bw_sta)
                ss, vs = self.bump("bw_sta", 16)
                s.dma_start(dstt[r0:r0 + rh, c0:c0 + cw], acc).then_inc(ss, 16)
                self.bw_sta = vs
                self.bw_cur = None
            else:
                self.bw_step += 1
        return

    def _tr_store(self):
        if self.tr_pend is None:
            return
        g = self.q["gpsimd"]
        p, dst_ap, ck, sl, vl = self.tr_pend
        self.tr_pend = None
        g.wait_ge(sl, vl)
        sst, vst = self.bump(f"tr_st{p}", 16)
        g.dma_start(dst_ap, self.tr_buf[p][:, 0:ck]).then_inc(sst, 16)
        self.tr_st[p] = vst

    def tr_open(self):
        self.tr_stack = ExitStack()
        self._tr_gen = getattr(self, "_tr_gen", 0) + 1
        self.tr_buf = [self.sbt(self.tr_stack, f"tr_b{i}_{self._tr_gen}", [128, 1024], BF16,
                                is_transpose_dest=True) for i in range(2)]

    def tr_close(self):
        self._tr_store()
        self.tr_stack.close()
        self.tr_stack = None

    def emit_tr(self, n, margin=6):
        if self.tr_stack is None:
            return
        s = self.q["sync"]
        for _ in range(n):
            if self.tr_next >= len(self.tr_jobs):
                break
            need, src_ap, dst_ap, ck = self.tr_jobs[self.tr_next]
            if need + margin >= self.bw_next:
                break
            ji = self.tr_next
            self.tr_next += 1
            p = ji % 2
            s.wait_ge(self.sem("bw_sta"), 16 * (need + 1))
            if self.tr_st[p]:
                s.wait_ge(self.sem(f"tr_st{p}"), self.tr_st[p])
            sl, vl = self.bump(f"tr_ltb{p}", 16)
            s.dma_start(self.tr_buf[p][:, 0:ck], src_ap, transpose=True).then_inc(sl, 16)
            self._tr_store()
            self.tr_pend = (p, dst_ap, ck, sl, vl)
        return

    def p1(self):
        self.dma_entry_fence()
        c = self.cfg
        t, v, g, s = self.q["tensor"], self.q["vector"], self.q["gpsimd"], self.q["sync"]
        a = self.q["scalar"]
        NE, NL, TB = c.NE, c.NL, c.TB
        NJ = NE + NL
        st = ExitStack()
        wq_sb = self.sbt(st, "p1_wq", [128, NE * c.E], BF16)
        wc_sb = self.sbt(st, "p1_wc", [128, NE * c.DL], BF16)
        xt = [self.sbt(st, f"p1_xt{i}", [128, NE * 512], BF16, is_transpose_dest=True) for i in range(2)]
        ev = [self.sbt(st, f"p1_ev{i}", [128, 512], BF16) for i in range(4)]
        pp = [self.pst(st, f"p1_ps{i}", [128, 512]) for i in range(2)]

        sw, vw = self.bump("p1_ldw", 32)
        g.dma_start(wq_sb, self.wqT.rearrange("(a p) c -> p a c", p=128)).then_inc(sw, 16)
        g.dma_start(wc_sb, self.wdkvT.rearrange("(a p) c -> p a c", p=128)).then_inc(sw, 16)

        xcbuf = [self.sbt(st, f"p1_xc{i}", [128, c.E], BF16) for i in range(4)]
        xt_free = [0, 0]
        ev_st = [0, 0, 0, 0]
        for tb in range(TB):
            p = tb % 2
            for ri in range(4):
                i = tb * 4 + ri
                pb = i % 4
                if i >= 4:
                    g.wait_ge(self.sem(f"p1_stx{pb}"), 16 * (i // 4))
                if self.cur(f"p1_ld{pb}"):
                    g.wait_ge(self.sem(f"p1_ld{pb}"), self.cur(f"p1_ld{pb}"))
                slc, vlc2 = self.bump(f"p1_ld{pb}", 16)
                g.dma_start(xcbuf[pb], self.x[i * 128:(i + 1) * 128, :]).then_inc(slc, 16)
                s.wait_ge(slc, vlc2)
                if i >= 4:
                    s.wait_ge(self.sem(f"p1_stx{pb}"), 16 * (i // 4))
                ssx, _ = self.bump(f"p1_stx{pb}", 16)
                s.dma_start(self.xb[i * 128:(i + 1) * 128, :], xcbuf[pb]).then_inc(ssx, 16)
            if xt_free[p]:
                a.wait_ge(self.sem("p1_mm"), xt_free[p])
            for pb in range(4):
                a.wait_ge(self.sem(f"p1_stx{pb}"), self.cur(f"p1_stx{pb}"))
            if self.cur(f"p1_ldx{p}"):
                a.wait_ge(self.sem(f"p1_ldx{p}"), self.cur(f"p1_ldx{p}"))
            slx, vlx = self.bump(f"p1_ldx{p}", 16 * NE)
            for e_i in range(NE):
                a.dma_start(xt[p][:, e_i * 512:(e_i + 1) * 512],
                            self.xb[tb * 512:(tb + 1) * 512, e_i * 128:(e_i + 1) * 128],
                            transpose=True).then_inc(slx, 16)
            t.wait_ge(slx, vlx)
            if tb == 0:
                t.wait_ge(sw, vw)
                t.wait_ge(self.sem("bw_vc"), self.bw_lut_ready)
            for j in range(NJ):
                pj = pp[j % 2]
                need = self.cur(f"p1_evv{j % 2}")
                if need:
                    t.wait_ge(self.sem(f"p1_evv{j % 2}"), need)
                for e_i in range(NE):
                    if j < NE:
                        lhs = wq_sb[:, e_i * c.E + j * 128: e_i * c.E + (j + 1) * 128]
                    else:
                        jj = j - NE
                        lhs = wc_sb[:, e_i * c.DL + jj * 128: e_i * c.DL + (jj + 1) * 128]
                    mi = t.matmul(pj, lhs, xt[p][:, e_i * 512:(e_i + 1) * 512],
                                  start=(e_i == 0), stop=(e_i == NE - 1))
                smm, vmm = self.bump("p1_mm")
                mi.then_inc(smm, 1)
                v.wait_ge(smm, vmm)
                k = j % 4
                if ev_st[k]:
                    v.wait_ge(self.sem(f"p1_st{k}"), ev_st[k])
                sev, vev = self.bump(f"p1_evv{j % 2}")
                v.tensor_copy(ev[k], pj).then_inc(sev, 1)
                s.wait_ge(sev, vev)
                if ev_st[k]:
                    s.wait_ge(self.sem(f"p1_st{k}"), ev_st[k])
                sst, vst = self.bump(f"p1_st{k}", 16)
                dst = (self.QT0[j * 128:(j + 1) * 128, tb * 512:(tb + 1) * 512] if j < NE
                       else self.CT0[(j - NE) * 128:(j - NE + 1) * 128, tb * 512:(tb + 1) * 512])
                s.dma_start(dst, ev[k]).then_inc(sst, 16)
                ev_st[k] = vst
                self.emit_build_ops(3)
            xt_free[p] = self.cur("p1_mm")
        self.p1_done = self.phase_done("p1")
        self.last_done = self.p1_done
        st.close()

    def p2(self):
        self.dma_entry_fence()
        self.tr_open()
        c = self.cfg
        t, v, sc, g, s = (self.q["tensor"], self.q["vector"], self.q["scalar"],
                          self.q["gpsimd"], self.q["sync"])
        NE, NL, NT, TB = c.NE, c.NL, c.NT, c.TB
        st = ExitStack()
        ct_sb = self.sbt(st, "p2_ct", [128, NL * c.T], BF16)
        kt_sb = self.sbt(st, "p2_kt", [128, c.T], BF16)
        vv_sb = self.sbt(st, "p2_vv", [128, NT * DH], BF16)
        qt_sb = self.sbt(st, "p2_qt", [128, c.T], BF16)
        sq_sb = self.sbt(st, "p2_sq", [128, c.T], BF16)
        kup_sb = self.sbt(st, "p2_kup", [128, NL * DH], BF16)
        vup_sb = self.sbt(st, "p2_vup", [128, NL * DH], BF16)
        p_sb = [self.sbt(st, f"p2_p{i}", [128, NT * 512], BF16) for i in range(2)]
        o_sb = [self.sbt(st, f"p2_o{i}", [128, 512], BF16) for i in range(2)]
        r_sb = [self.sbt(st, f"p2_r{i}", [1, 512], F32) for i in range(2)]
        lnt_sb = [self.sbt(st, f"p2_lnt{i}", [1, 512], F32) for i in range(2)]
        br_sb = self.sbt(st, "p2_br", [128, 512], F32)
        mx_sb = self.sbt(st, "p2_mx", [1, 16], F32)
        biasb = self.sbt(st, "p2_bias", [128, 1], F32)

        sp_pair = [self.pst(st, f"p2_s{i}", [128, 1024]) for i in range(2)]
        sp = [sp_pair[0][:, 0:512], sp_pair[1][:, 0:512]]
        op_ps = [self.pst(st, f"p2_op{i}", [128, 512]) for i in range(2)]
        aux_ps = self.pst(st, "p2_aux", [128, 512])
        sum_ps = [aux_ps[0:1, :], aux_ps[32:33, :]]
        nrm_ps = aux_ps[64:65, :]
        br_ps = self.pst(st, "p2_brp", [128, 512])
        nrm2_ps = br_ps[0:1, :]
        kv_ps = [op_ps[0][:, 0:DH], op_ps[1][:, 0:DH]]

        if self.cur("p2_ldc"):
            s.wait_ge(self.sem("p2_ldc"), self.cur("p2_ldc"))
        slc, vlc = self.bump("p2_ldc", 16 * NL)
        self.fence(s, self.p1_done)
        self.fence(g, self.p1_done)
        for l_i in range(NL):
            s.dma_start(ct_sb[:, l_i * c.T:(l_i + 1) * c.T],
                        self.CT0[l_i * 128:(l_i + 1) * 128, :]).then_inc(slc, 16)
        t.wait_ge(slc, vlc)
        t.wait_ge(self.sem(self.const_ready[0]), self.const_ready[1])

        nchunk = c.T // 512
        o_st = [0, 0]
        rv_used = [0, 0]
        nrm_rd = [0, 0]
        pvmm = [0, 0]
        for h in range(c.NH):
            mm_prev = self.cur("p2_mm")
            ac_prev = self.cur("p2_ac")
            g.wait_ge(self.sem("p2_mm"), mm_prev)
            swh, vwh = self.bump("p2_ldw", 32)
            g.dma_start(kup_sb, self.wupkT[h].rearrange("(a p) c -> p a c", p=128)).then_inc(swh, 16)
            g.dma_start(vup_sb, self.wupvT[h].rearrange("(a p) c -> p a c", p=128)).then_inc(swh, 16)
            if self.cur("p2_ldq"):
                g.wait_ge(self.sem("p2_ldq"), self.cur("p2_ldq"))
            sqh, vqh = self.bump("p2_ldq", 16)
            g.dma_start(qt_sb, self.QT0[h * 128:(h + 1) * 128, :]).then_inc(sqh, 16)

            t.wait_ge(swh, vwh)
            t.wait_ge(self.sem("p2_ac"), ac_prev)
            for tb in range(TB):
                pj = sp[tb % 2]
                if tb >= 2:
                    t.wait_ge(self.sem("p2_kte"), self.cur("p2_kte") - 1)
                for l_i in range(NL):
                    mi = t.matmul(pj, kup_sb[:, l_i * DH:(l_i + 1) * DH],
                                  ct_sb[:, l_i * c.T + tb * 512: l_i * c.T + (tb + 1) * 512],
                                  start=(l_i == 0), stop=(l_i == NL - 1))
                smm, vmm = self.bump("p2_mm")
                mi.then_inc(smm, 1)
                v.wait_ge(smm, vmm)
                if h > 0 and tb == 0:
                    v.wait_ge(self.sem("p2_mm"), mm_prev)
                ske, _ = self.bump("p2_kte")
                v.tensor_copy(kt_sb[:, tb * 512:(tb + 1) * 512], pj).then_inc(ske, 1)
            if self.cur("p2_pv0"):
                t.wait_ge(self.sem("p2_pv0"), self.cur("p2_pv0"))
            if self.cur("p2_pv1"):
                t.wait_ge(self.sem("p2_pv1"), self.cur("p2_pv1"))
            for tt in range(NT):
                pk = kv_ps[tt % 2]
                if tt >= 2:
                    t.wait_ge(self.sem("p2_ve"), self.cur("p2_ve") - 1)
                for l_i in range(NL):
                    mi = t.matmul(pk,
                                  ct_sb[:, l_i * c.T + tt * 128: l_i * c.T + (tt + 1) * 128],
                                  vup_sb[:, l_i * DH:(l_i + 1) * DH],
                                  start=(l_i == 0), stop=(l_i == NL - 1))
                smm, vmm = self.bump("p2_mm")
                mi.then_inc(smm, 1)
                v.wait_ge(smm, vmm)
                if h > 0 and tt == 0:
                    v.wait_ge(self.sem("p2_mm"), mm_prev)
                sve, _ = self.bump("p2_ve")
                v.tensor_copy(vv_sb[:, tt * DH:(tt + 1) * DH], pk).then_inc(sve, 1)

            sc.wait_ge(sqh, vqh)
            if h > 0:
                sc.wait_ge(self.sem("p2_mm"), mm_prev)
            sa, va = self.bump("p2_ac")
            sc.activation(sq_sb, qt_sb, AF.Square).then_inc(sa, 1)
            t.wait_ge(self.sem("p2_vc"), self.cur("p2_vc"))
            for ci in range(nchunk):
                t.wait_ge(sa, va)
                bk = ci % 2
                nb = nrm_ps if bk == 0 else nrm2_ps
                if nrm_rd[bk]:
                    t.wait_ge(self.sem("p2_nrx"), nrm_rd[bk])
                mi = t.matmul(nb, self.ones_sb, sq_sb[:, ci * 512:(ci + 1) * 512],
                              start=True, stop=True)
                smm, vmm = self.bump("p2_mm")
                mi.then_inc(smm, 1)
                v.wait_ge(smm, vmm)
                sx, vx = self.bump("p2_nrx")
                v.reduce_max(mx_sb[:, ci:ci + 1], nb, axis=AX).then_inc(sx, 1)
                nrm_rd[bk] = vx
            ksq_gate = self.cur("p2_mm")
            sc.wait_ge(self.sem("p2_mm"), ksq_gate)
            sc.wait_ge(self.sem("p2_kte"), self.cur("p2_kte"))
            ska, vka = self.bump("p2_ac")
            sc.activation(sq_sb, kt_sb, AF.Square).then_inc(ska, 1)
            for ci in range(nchunk):
                t.wait_ge(ska, vka)
                bk = ci % 2
                nb = nrm_ps if bk == 0 else nrm2_ps
                if nrm_rd[bk]:
                    t.wait_ge(self.sem("p2_nrx"), nrm_rd[bk])
                mi = t.matmul(nb, self.ones_sb, sq_sb[:, ci * 512:(ci + 1) * 512],
                              start=True, stop=True)
                smm, vmm = self.bump("p2_mm")
                mi.then_inc(smm, 1)
                v.wait_ge(smm, vmm)
                sx, vx = self.bump("p2_nrx")
                v.reduce_max(mx_sb[:, 8 + ci:8 + ci + 1], nb, axis=AX).then_inc(sx, 1)
                nrm_rd[bk] = vx
            v.wait_ge(self.sem("p2_nrx"), self.cur("p2_nrx"))
            sv1, _ = self.bump("p2_vc")
            v.reduce_max(mx_sb[:, 14:15], mx_sb[:, 0:nchunk], axis=AX).then_inc(sv1, 1)
            sv2, vv2 = self.bump("p2_vc")
            v.reduce_max(mx_sb[:, 15:16], mx_sb[:, 8:8 + nchunk], axis=AX).then_inc(sv2, 1)
            v.wait_ge(sv2, vv2)
            sv3, vv3 = self.bump("p2_vc")
            v.tensor_tensor(mx_sb[:, 13:14], mx_sb[:, 14:15], mx_sb[:, 15:16], ALU.mult).then_inc(sv3, 1)
            sc.wait_ge(sv3, vv3)
            sa4, va4 = self.bump("p2_ac")
            sc.activation(mx_sb[:, 12:13], mx_sb[:, 13:14], AF.Sqrt).then_inc(sa4, 1)
            sc.wait_ge(sa4, va4)
            sa5, va5 = self.bump("p2_ac")
            sc.mul(mx_sb[:, 11:12], mx_sb[:, 12:13], -1.01 * ISCALE).then_inc(sa5, 1)
            t.wait_ge(sa5, va5)
            mi = t.matmul(br_ps[:, 0:1], self.onesr_sb, mx_sb[:, 11:12], start=True, stop=True)
            smm, vmm = self.bump("p2_mm")
            mi.then_inc(smm, 1)
            sc.wait_ge(smm, vmm)
            sbb, vbb = self.bump("p2_bv")
            sc.copy(biasb, br_ps[:, 0:1]).then_inc(sbb, 1)
            sc.wait_ge(sbb, vbb)

            t.wait_ge(self.sem("p2_kte"), self.cur("p2_kte"))
            pend = None
            prev = None
            br_ev = [0]

            def fin2(qb_, rv_):
                pqb_ = qb_ % 2
                t.wait_ge(self.sem("p2_ac"), rv_)
                if br_ev[0]:
                    t.wait_ge(self.sem("p2_vc"), br_ev[0])
                mi_ = t.matmul(br_ps, self.onesr_sb, r_sb[pqb_], start=True, stop=True)
                smm_, vmm_ = self.bump("p2_mm")
                mi_.then_inc(smm_, 1)
                v.wait_ge(smm_, vmm_)
                for pp_ in range(2):
                    if self.cur(f"p2_pv{pp_}"):
                        v.wait_ge(self.sem(f"p2_pv{pp_}"), self.cur(f"p2_pv{pp_}"))
                se_, ve_ = self.bump("p2_vc")
                v.tensor_copy(br_sb, br_ps).then_inc(se_, 1)
                br_ev[0] = ve_
                v.wait_ge(se_, ve_)
                if o_st[pqb_]:
                    v.wait_ge(self.sem(f"p2_st{pqb_}"), o_st[pqb_])
                so_, vo_ = self.bump(f"p2_pv{pqb_}")
                v.tensor_tensor(o_sb[pqb_], op_ps[pqb_], br_sb, ALU.mult).then_inc(so_, 1)
                s.wait_ge(so_, vo_)
                if o_st[pqb_]:
                    s.wait_ge(self.sem(f"p2_st{pqb_}"), o_st[pqb_])
                sst_, vst_ = self.bump(f"p2_st{pqb_}", 16)
                s.dma_start(self.OT0[h * 128:(h + 1) * 128, qb_ * 512:(qb_ + 1) * 512],
                            o_sb[pqb_]).then_inc(sst_, 16)
                o_st[pqb_] = vst_

            def pv_tile(qb_, eb_, tt_, first, last):
                pq_ = qb_ % 2
                if first:
                    if self.cur(f"p2_pv{pq_}"):
                        t.wait_ge(self.sem(f"p2_pv{pq_}"), self.cur(f"p2_pv{pq_}"))
                    if rv_used[pq_]:
                        t.wait_ge(self.sem("p2_ac"), rv_used[pq_])
                t.matmul(op_ps[pq_], vv_sb[:, tt_ * DH:(tt_ + 1) * DH],
                         p_sb[pq_][:, tt_ * 512:(tt_ + 1) * 512],
                         start=first, stop=last, skip_group_check=True)
                mi_ = t.matmul(sum_ps[pq_], self.ones_sb, p_sb[pq_][:, tt_ * 512:(tt_ + 1) * 512],
                               start=first, stop=last, skip_group_check=True)
                if last:
                    smm_, vmm_ = self.bump("p2_mm")
                    mi_.then_inc(smm_, 1)
                    return vmm_
                return None

            def b_finalize(qb_, vmm_):
                nonlocal pend
                pq_ = qb_ % 2
                sc.wait_ge(self.sem("p2_mm"), vmm_)
                sr, vr = self.bump("p2_ac")
                sc.activation(lnt_sb[pq_], sum_ps[pq_], AF.Ln).then_inc(sr, 1)
                sc.wait_ge(sr, vr)
                sr2, vr2 = self.bump("p2_ac")
                sc.activation(r_sb[pq_], lnt_sb[pq_], AF.Exp, scale=-1.0).then_inc(sr2, 1)
                rv_used[pq_] = vr2
                pvmm[pq_] = vmm_
                if pend is not None:
                    fin2(*pend)
                pend = (qb_, vr2)

            NP = NT // 2
            for qb in range(TB):
                pqb = qb % 2
                if pvmm[pqb]:
                    sc.wait_ge(self.sem("p2_mm"), pvmm[pqb])
                exp_base = self.cur("p2_ac")
                bvmm = None
                for pr in range(NP):
                    tt0 = 2 * pr
                    pj = sp_pair[pr % 2]
                    if pr >= 2:
                        t.wait_ge(self.sem("p2_ac"), exp_base + pr - 1)
                    elif pr == 0 and prev is not None:
                        t.wait_ge(self.sem("p2_ac"), prev[1] + NP)
                    mi = t.matmul(pj[:, 0:512], kt_sb[:, tt0 * 128:(tt0 + 1) * 128],
                                  qt_sb[:, qb * 512:(qb + 1) * 512], start=True, stop=True)
                    mi = t.matmul(pj[:, 512:1024], kt_sb[:, (tt0 + 1) * 128:(tt0 + 2) * 128],
                                  qt_sb[:, qb * 512:(qb + 1) * 512], start=True, stop=True)
                    smm, vmm = self.bump("p2_mm")
                    mi.then_inc(smm, 1)
                    sc.wait_ge(smm, vmm)
                    sae, _ = self.bump("p2_ac")
                    sc.activation(p_sb[pqb][:, tt0 * 512:(tt0 + 2) * 512], pj, AF.Exp,
                                  bias=biasb, scale=ISCALE).then_inc(sae, 1)
                    if prev is not None:
                        for d_ in (0, 1):
                            ttp = tt0 + d_
                            r_ = pv_tile(prev[0], prev[1], ttp, ttp == 0, ttp == NT - 1)
                            if r_ is not None:
                                bvmm = r_
                    self.emit_build_ops(2)
                if prev is not None:
                    b_finalize(prev[0], bvmm)
                prev = (qb, exp_base)
                self.emit_tr(1)
            qb_, eb_ = prev
            bvmm = None
            for tt in range(NT):
                if tt % 2 == 0:
                    t.wait_ge(self.sem("p2_ac"), eb_ + tt // 2 + 1)
                r_ = pv_tile(qb_, eb_, tt, tt == 0, tt == NT - 1)
                if r_ is not None:
                    bvmm = r_
            b_finalize(qb_, bvmm)
            if pend is not None:
                fin2(*pend)
                pend = None
            prev = None
            self.emit_tr(1)
        self.p2_done = self.phase_done("p2")
        self.last_done = self.p2_done
        st.close()
        self.tr_close()

    def p3(self):
        self.dma_entry_fence()
        c = self.cfg
        t, v, g, s = self.q["tensor"], self.q["vector"], self.q["gpsimd"], self.q["sync"]
        a = self.q["scalar"]
        NE, TB = c.NE, c.TB
        st = ExitStack()
        wo_sb = self.sbt(st, "p3_wo", [128, NE * c.E], BF16)
        ot = [self.sbt(st, f"p3_ot{i}", [128, NE * 512], BF16) for i in range(2)]
        xt = [self.sbt(st, "p3_xt0", [128, NE * 512], BF16, is_transpose_dest=True)]
        xt = [xt[0], xt[0]]
        ap = [self.pst(st, f"p3_ap{i}", [128, 512]) for i in range(3)]
        sum_ps = self.pst(st, "p3_sum", [1, 512])
        sq_ps = self.pst(st, "p3_sq", [1, 512])
        mb_ps = self.pst(st, "p3_mbp", [128, 512])
        rb_ps = self.pst(st, "p3_rbp", [128, 512])
        lnr = self.make_ln_res(st, "p3", sum_ps, sq_ps, mb_ps, rb_ps, BF16, depth=2)

        if self.cur("p3_ldw"):
            g.wait_ge(self.sem("p3_ldw"), self.cur("p3_ldw"))
        sw, vw = self.bump("p3_ldw", 16)
        self.fence(g, self.p2_done)
        g.dma_start(wo_sb, self.woT.rearrange("(a p) c -> p a c", p=128)).then_inc(sw, 16)

        ot_free = [0, 0]
        xt_free = [0, 0]
        self._p3_xt_last = None
        self.fence(s, self.p2_done)
        self.fence(a, self.p2_done)
        ld_vals = {}

        def emit_loads(tb):
            p = tb % 2
            if ot_free[p]:
                g.wait_ge(self.sem("p3_mm"), ot_free[p])
            if self.cur(f"p3_ldo{p}"):
                g.wait_ge(self.sem(f"p3_ldo{p}"), self.cur(f"p3_ldo{p}"))
            slo, vlo = self.bump(f"p3_ldo{p}", 16 * NE)
            for e_i in range(NE):
                g.dma_start(ot[p][:, e_i * 512:(e_i + 1) * 512],
                            self.OT0[e_i * 128:(e_i + 1) * 128,
                                     tb * 512:(tb + 1) * 512]).then_inc(slo, 16)
            for bi in range(3):
                if self.cur(f"p3_zadd{bi}"):
                    s.wait_ge(self.sem(f"p3_zadd{bi}"), self.cur(f"p3_zadd{bi}"))
            if self.cur("p3_ltb0"):
                s.wait_ge(self.sem("p3_ltb0"), self.cur("p3_ltb0"))
            slx, vlx = self.bump("p3_ltb0", 16 * NE)
            for e_i in range(NE):
                s.dma_start(xt[p][:, e_i * 512:(e_i + 1) * 512],
                            self.xb[tb * 512:(tb + 1) * 512, e_i * 128:(e_i + 1) * 128],
                            transpose=True).then_inc(slx, 16)
            ld_vals[tb] = (slo, vlo, slx, vlx)

        emit_loads(0)
        for tb in range(TB):
            p = tb % 2
            slo, vlo, slx, vlx = ld_vals[tb]
            t.wait_ge(slo, vlo)
            if tb == 0:
                t.wait_ge(sw, vw)
            v.wait_ge(slx, vlx)
            lnr["cur_par"] = tb % 2
            for j in range(NE):
                bk = j % 3
                need = self.cur(f"p3_zadd{bk}")
                if need:
                    t.wait_ge(self.sem(f"p3_zadd{bk}"), need)
                for e_i in range(NE):
                    mi = t.matmul(ap[bk],
                                  wo_sb[:, e_i * c.E + j * 128: e_i * c.E + (j + 1) * 128],
                                  ot[p][:, e_i * 512:(e_i + 1) * 512],
                                  start=(e_i == 0), stop=(e_i == NE - 1), skip_group_check=True)
                smm, vmm = self.bump("p3_mm")
                mi.then_inc(smm, 1)
                self.ln_z("p3", lnr, j, ap[bk], [(smm, vmm)],
                          xt[p][:, j * 512:(j + 1) * 512], bank=bk)
                self.emit_build_ops(4)
            if tb + 1 < TB:
                emit_loads(tb + 1)
            if tb > 0:
                lnr["norm_par"] = (tb - 1) % 2
                self.ln_norm("p3", lnr, tb - 1, self.X1)
            self.ln_stats("p3", lnr, tb)
            ot_free[p] = self.cur("p3_mm")
            xt_free[p] = (self.cur("p3_zadd0"), self.cur("p3_zadd1"), self.cur("p3_zadd2"))
            self._p3_xt_last = xt_free[p]
        lnr["norm_par"] = (TB - 1) % 2
        self.ln_norm("p3", lnr, TB - 1, self.X1)
        self.p3_done = self.phase_done("p3")
        self.last_done = self.p3_done
        st.close()

    def p4(self):
        self.dma_entry_fence()
        c = self.cfg
        t, v, sc, g, s = (self.q["tensor"], self.q["vector"], self.q["scalar"],
                          self.q["gpsimd"], self.q["sync"])
        NE, TB = c.NE, c.TB
        st = ExitStack()
        wf_sb = self.sbt(st, "p4_wf", [128, NE * c.E], BF16)
        aw_sb = self.sbt(st, "p4_aw", [128, NE * 128], BF16)
        bw_sb = self.sbt(st, "p4_bw", [128, c.E], BF16)
        gw_sb = self.sbt(st, "p4_gw", [128, NE * NEXP], BF16)
        x1t = [self.sbt(st, f"p4_x1{i}", [128, NE * 512], BF16) for i in range(2)]
        lg_sb = self.sbt(st, "p4_lg", [128, 4 * NEXP], F32)
        gsc = self.sbt(st, "p4_gsc", [128, 16], F32)
        exg_sb = self.sbt(st, "p4_exg", [128, NEXP], F32)
        cwn_sb = self.sbt(st, "p4_cwn", [128, NEXP], BF16)
        cwt_sb = self.sbt(st, "p4_cwt", [NEXP, 512], BF16)
        cwx_sb = self.sbt(st, "p4_cwx", [128, 512], BF16)
        t1s_sb = self.sbt(st, "p4_t1s", [128, 512], BF16)

        out_ps = [self.pst(st, f"p4_out{i}", [128, 512]) for i in range(3)]
        t1_ps = self.pst(st, "p4_t1", [128, 512])
        u_ps = self.pst(st, "p4_u", [128, 512])
        u_bf = self.pst(st, "p4_ubf", [128, 512], BF16)
        sum_ps = self.pst(st, "p4_sum", [1, 512])
        sq_ps = self.pst(st, "p4_sq", [1, 512])
        lnr = self.make_ln_res(st, "p4", sum_ps, sq_ps, t1_ps, u_ps, BF16, depth=2)

        if self.cur("p4_ldw"):
            g.wait_ge(self.sem("p4_ldw"), self.cur("p4_ldw"))
        sw, vw = self.bump("p4_ldw", 16 * 4)
        self.fence(g, self.p3_done)
        g.dma_start(wf_sb, self.wffT.rearrange("(a p) c -> p a c", p=128)).then_inc(sw, 16)
        g.dma_start(aw_sb, self.awT.rearrange("(a p) c -> p a c", p=128)).then_inc(sw, 16)
        g.dma_start(bw_sb, self.bwT).then_inc(sw, 16)
        g.dma_start(gw_sb, self.gatewT.rearrange("(a p) c -> p a c", p=128)).then_inc(sw, 16)

        x1_free = [0, 0]
        self.fence(s, self.p3_done)
        for tb in range(TB):
            p = tb % 2
            if x1_free[p]:
                g.wait_ge(self.sem("p4_mm"), x1_free[p][0])
                for bi in range(3):
                    if x1_free[p][1 + bi]:
                        g.wait_ge(self.sem(f"p4_zadd{bi}"), x1_free[p][1 + bi])
            if self.cur(f"p4_ldo{p}"):
                g.wait_ge(self.sem(f"p4_ldo{p}"), self.cur(f"p4_ldo{p}"))
            slx, vlx = self.bump(f"p4_ldo{p}", 16 * NE)
            for e_i in range(NE):
                g.dma_start(x1t[p][:, e_i * 512:(e_i + 1) * 512],
                            self.X1[e_i * 128:(e_i + 1) * 128,
                                    tb * 512:(tb + 1) * 512]).then_inc(slx, 16)
            t.wait_ge(slx, vlx)
            if tb == 0:
                t.wait_ge(sw, vw)
                t.wait_ge(self.sem(self.const_ready[0]), self.const_ready[1])

            if getattr(self, "_p4_ups_free", 0):
                t.wait_ge(self.sem("p4_vc"), self._p4_ups_free)
            if lnr.get("free_wait"):
                t.wait_ge(*lnr["free_wait"])
            for ts in range(4):
                for e_i in range(NE):
                    mi = t.matmul(u_ps[:, ts * NEXP:(ts + 1) * NEXP],
                                  x1t[p][:, e_i * 512 + ts * 128: e_i * 512 + (ts + 1) * 128],
                                  gw_sb[:, e_i * NEXP:(e_i + 1) * NEXP],
                                  start=(e_i == 0), stop=(e_i == NE - 1))
            smm, vmm = self.bump("p4_mm")
            mi.then_inc(smm, 1)
            v.wait_ge(smm, vmm)
            if getattr(self, "_p4_lg_readers", 0):
                v.wait_ge(self.sem("p4_ac"), self._p4_lg_readers)
            sv, vv = self.bump("p4_vc")
            v.tensor_copy(lg_sb, u_ps[:, 0:4 * NEXP]).then_inc(sv, 1)
            for ts in range(4):
                lg = lg_sb[:, ts * NEXP:(ts + 1) * NEXP]
                v.wait_ge(sv, vv)
                s1, v1 = self.bump("p4_vc")
                v.reduce_max(gsc[:, 0:1], lg, axis=AX).then_inc(s1, 1)
                v.wait_ge(s1, v1)
                s2, v2 = self.bump("p4_vc")
                v.tensor_scalar(gsc[:, 2:3], gsc[:, 0:1], -1.0, None, ALU.mult).then_inc(s2, 1)
                s3, v3 = self.bump("p4_vc")
                v.tensor_scalar(exg_sb, lg, gsc[:, 0:1], -1e30, ALU.is_equal, ALU.mult).then_inc(s3, 1)
                v.wait_ge(s3, v3)
                s4, v4 = self.bump("p4_vc")
                v.tensor_tensor(exg_sb, lg, exg_sb, ALU.add).then_inc(s4, 1)
                v.wait_ge(s4, v4)
                s5, v5 = self.bump("p4_vc")
                v.reduce_max(gsc[:, 1:2], exg_sb, axis=AX).then_inc(s5, 1)
                sc.wait_ge(s5, v5)
                sa1, va1 = self.bump("p4_ac")
                sc.activation(exg_sb, lg, AF.Exp, bias=gsc[:, 2:3]).then_inc(sa1, 1)
                v.wait_ge(sa1, va1)
                if getattr(self, "_p4_tr_last", 0):
                    v.wait_ge(self.sem("p4_mm"), self._p4_tr_last)
                s6, v6 = self.bump("p4_vc")
                v.tensor_scalar(cwn_sb, lg, gsc[:, 1:2], None, ALU.is_ge).then_inc(s6, 1)
                v.wait_ge(s6, v6)
                s7, v7 = self.bump("p4_vc")
                v.tensor_tensor(exg_sb, exg_sb, cwn_sb, ALU.mult).then_inc(s7, 1)
                v.wait_ge(s7, v7)
                s8, v8 = self.bump("p4_vc")
                v.reduce_sum(gsc[:, 3:4], exg_sb, axis=AX).then_inc(s8, 1)
                v.wait_ge(s8, v8)
                s9, v9 = self.bump("p4_vc")
                v.reciprocal(gsc[:, 4:5], gsc[:, 3:4]).then_inc(s9, 1)
                v.wait_ge(s9, v9)
                s10, v10 = self.bump("p4_vc")
                v.tensor_scalar(cwn_sb, exg_sb, gsc[:, 4:5], None, ALU.mult).then_inc(s10, 1)
                t.wait_ge(s10, v10)
                if self.cur("p4_vg") > 0:
                    t.wait_ge(self.sem("p4_vg"), self.cur("p4_vg"))
                mi = t.transpose(u_bf[0:NEXP, ts * 128:(ts + 1) * 128], cwn_sb, self.id_sb)
                smm, vmm = self.bump("p4_mm")
                mi.then_inc(smm, 1)
                self._p4_tr_last = vmm
                v.wait_ge(smm, vmm)
                s11, v11 = self.bump("p4_vg")
                v.tensor_copy(cwt_sb[:, ts * 128:(ts + 1) * 128],
                              u_bf[0:NEXP, ts * 128:(ts + 1) * 128]).then_inc(s11, 1)
            v16 = self.cur("p4_vg")
            t.wait_ge(self.sem("p4_vg"), v16)
            mi = t.matmul(u_ps, self.e16_sb, cwt_sb, start=True, stop=True)
            smm, vmm = self.bump("p4_mm")
            mi.then_inc(smm, 1)
            need = self.cur("p4_t1e")
            if need:
                t.wait_ge(self.sem("p4_t1e"), need)
            for e_i in range(NE):
                mi2 = t.matmul(t1_ps, aw_sb[:, e_i * 128:(e_i + 1) * 128],
                               x1t[p][:, e_i * 512:(e_i + 1) * 512],
                               start=(e_i == 0), stop=(e_i == NE - 1))
            smm2, vmm2 = self.bump("p4_mm")
            mi2.then_inc(smm2, 1)
            v.wait_ge(smm, vmm)
            sx, vx = self.bump("p4_vc")
            v.tensor_copy(cwx_sb, u_ps).then_inc(sx, 1)
            self._p4_ups_free = vx
            self._p4_lg_readers = self.cur("p4_ac")
            v.wait_ge(smm2, vmm2)
            v.wait_ge(sx, vx)
            st1, vt1 = self.bump("p4_t1e")
            v.tensor_tensor(t1s_sb, t1_ps, cwx_sb, ALU.mult).then_inc(st1, 1)

            lnr["cur_par"] = tb % 2
            for j in range(NE):
                bk = j % 3
                need = self.cur(f"p4_zadd{bk}")
                if need:
                    t.wait_ge(self.sem(f"p4_zadd{bk}"), need)
                for e_i in range(NE):
                    t.matmul(out_ps[bk],
                             wf_sb[:, e_i * c.E + j * 128: e_i * c.E + (j + 1) * 128],
                             x1t[p][:, e_i * 512:(e_i + 1) * 512],
                             start=(e_i == 0), stop=False, skip_group_check=True)
                if j == 0:
                    t.wait_ge(st1, vt1)
                mi = t.matmul(out_ps[bk], bw_sb[:, j * 128:(j + 1) * 128], t1s_sb,
                              start=False, stop=True, skip_group_check=True)
                smm, vmm = self.bump("p4_mm")
                mi.then_inc(smm, 1)
                self.ln_z("p4", lnr, j, out_ps[bk], [(smm, vmm)],
                          x1t[p][:, j * 512:(j + 1) * 512], bank=bk)
                self.emit_build_ops(4)
            if tb > 0:
                lnr["norm_par"] = (tb - 1) % 2
                lnr["extra_norm_wait"] = [(st1, vt1),
                                          (self.sem("p4_vc"), self._p4_ups_free)]
                self.ln_norm("p4", lnr, tb - 1, self.X2)
            self.ln_stats("p4", lnr, tb)
            x1_free[p] = (self.cur("p4_mm"), self.cur("p4_zadd0"),
                          self.cur("p4_zadd1"), self.cur("p4_zadd2"))
        lnr["norm_par"] = (TB - 1) % 2
        lnr["extra_norm_wait"] = []
        self.ln_norm("p4", lnr, TB - 1, self.X2)
        self.p4_done = self.phase_done("p4")
        self.last_done = self.p4_done
        st.close()

    def p5(self):
        self.dma_entry_fence()
        self.tr_open()
        c = self.cfg
        t, v, g, s = self.q["tensor"], self.q["vector"], self.q["gpsimd"], self.q["sync"]
        NE, TB, NG = c.NE, c.TB, c.NG
        st = ExitStack()
        wq_sb = self.sbt(st, "p5_wq", [128, c.NH * NE * DH], BF16)
        wk_sb = self.sbt(st, "p5_wk", [128, NG * NE * DH], BF16)
        wv_sb = self.sbt(st, "p5_wv", [128, NG * NE * DH], BF16)
        bq_sb = self.sbt(st, "p5_bq", [128, c.NH], F32)
        bk_sb = self.sbt(st, "p5_bk", [128, NG], F32)
        bv_sb = self.sbt(st, "p5_bv", [128, NG], F32)
        x2t = [self.sbt(st, f"p5_x2{i}", [128, NE * 512], BF16) for i in range(2)]
        ev = [self.sbt(st, f"p5_ev{i}", [128, 512], BF16) for i in range(4)]
        vt_sb = self.sbt(st, "p5_vt", [128, 512], BF16)
        tv_sb = [self.sbt(st, f"p5_tv{i}", [128, 128], BF16) for i in range(2)]

        pp = [self.pst(st, f"p5_ps{i}", [128, 512]) for i in range(2)]
        tr_ps = [self.pst(st, f"p5_tr{i}", [128, 128], BF16) for i in range(2)]

        if self.cur("p5_ldw"):
            g.wait_ge(self.sem("p5_ldw"), self.cur("p5_ldw"))
        sw, vw = self.bump("p5_ldw", 16 * (c.NH + 2 * NG + 3))
        self.fence(g, self.p4_done)
        for h in range(c.NH):
            g.dma_start(wq_sb[:, h * NE * DH:(h + 1) * NE * DH],
                        self.wq1[h].rearrange("(a p) c -> p a c", p=128)).then_inc(sw, 16)
        for gg in range(NG):
            g.dma_start(wk_sb[:, gg * NE * DH:(gg + 1) * NE * DH],
                        self.wk1[gg].rearrange("(a p) c -> p a c", p=128)).then_inc(sw, 16)
            g.dma_start(wv_sb[:, gg * NE * DH:(gg + 1) * NE * DH],
                        self.wv1[gg].rearrange("(a p) c -> p a c", p=128)).then_inc(sw, 16)
        g.dma_start(bq_sb, self.bq1.rearrange("a p -> p a"),
                    allow_slow_non_contiguous=True).then_inc(sw, 16)
        g.dma_start(bk_sb, self.bk1.rearrange("a p -> p a"),
                    allow_slow_non_contiguous=True).then_inc(sw, 16)
        g.dma_start(bv_sb, self.bv1.rearrange("a p -> p a"),
                    allow_slow_non_contiguous=True).then_inc(sw, 16)

        x2_free = [0, 0]
        ev_st = [0, 0, 0, 0]
        tv_st = [0, 0]
        self.fence(s, self.p4_done)
        for tb in range(TB):
            p = tb % 2
            if x2_free[p]:
                g.wait_ge(self.sem("p5_mm"), x2_free[p])
            if self.cur(f"p5_ldo{p}"):
                g.wait_ge(self.sem(f"p5_ldo{p}"), self.cur(f"p5_ldo{p}"))
            slx, vlx = self.bump(f"p5_ldo{p}", 16 * NE)
            for e_i in range(NE):
                g.dma_start(x2t[p][:, e_i * 512:(e_i + 1) * 512],
                            self.X2[e_i * 128:(e_i + 1) * 128,
                                    tb * 512:(tb + 1) * 512]).then_inc(slx, 16)
            t.wait_ge(slx, vlx)
            if tb == 0:
                t.wait_ge(sw, vw)
            for kind in range(3):
                nout = c.NH if kind == 0 else NG
                wsrc = (wq_sb, wk_sb, wv_sb)[kind]
                bias = (bq_sb, bk_sb, bv_sb)[kind]
                dest = (self.QT1, self.KT1, None)[kind]
                for jo in range(nout):
                    pj = pp[jo % 2]
                    need = self.cur(f"p5_evv{jo % 2}")
                    if need:
                        t.wait_ge(self.sem(f"p5_evv{jo % 2}"), need)
                    for e_i in range(NE):
                        mi = t.matmul(pj, wsrc[:, jo * NE * DH + e_i * DH: jo * NE * DH + (e_i + 1) * DH],
                                      x2t[p][:, e_i * 512:(e_i + 1) * 512],
                                      start=(e_i == 0), stop=(e_i == NE - 1))
                    smm, vmm = self.bump("p5_mm")
                    mi.then_inc(smm, 1)
                    v.wait_ge(smm, vmm)
                    if kind < 2:
                        k = jo % 4
                        if ev_st[k]:
                            v.wait_ge(self.sem(f"p5_st{k}"), ev_st[k])
                        sev, vev = self.bump(f"p5_evv{jo % 2}")
                        v.tensor_scalar(ev[k], pj, bias[:, jo:jo + 1], None, ALU.add).then_inc(sev, 1)
                        s.wait_ge(sev, vev)
                        if ev_st[k]:
                            s.wait_ge(self.sem(f"p5_st{k}"), ev_st[k])
                        sst, vst = self.bump(f"p5_st{k}", 16)
                        s.dma_start(dest[jo * 128:(jo + 1) * 128, tb * 512:(tb + 1) * 512],
                                    ev[k]).then_inc(sst, 16)
                        ev_st[k] = vst
                        self.emit_build_ops(3)
                    else:
                        if getattr(self, "_p5_vt_free", 0):
                            v.wait_ge(self.sem("p5_mm"), self._p5_vt_free)
                        sev, vev = self.bump(f"p5_evv{jo % 2}")
                        v.tensor_scalar(vt_sb, pj, bias[:, jo:jo + 1], None, ALU.add).then_inc(sev, 1)
                        t.wait_ge(sev, vev)
                        for ts in range(4):
                            ptr = tr_ps[ts % 2]
                            need3 = self.cur("p5_tre") - 1 if ts >= 2 else 0
                            if need3 > 0:
                                t.wait_ge(self.sem("p5_tre"), need3)
                            mi = t.transpose(ptr, vt_sb[:, ts * 128:(ts + 1) * 128], self.id_sb)
                            smm, vmm = self.bump("p5_mm")
                            mi.then_inc(smm, 1)
                            v.wait_ge(smm, vmm)
                            if tv_st[ts % 2]:
                                v.wait_ge(self.sem(f"p5_sv{ts % 2}"), tv_st[ts % 2])
                            ste, vte = self.bump("p5_tre")
                            v.tensor_copy(tv_sb[ts % 2], ptr).then_inc(ste, 1)
                            s.wait_ge(ste, vte)
                            if tv_st[ts % 2]:
                                s.wait_ge(self.sem(f"p5_sv{ts % 2}"), tv_st[ts % 2])
                            sst, vst = self.bump(f"p5_sv{ts % 2}", 16)
                            s.dma_start(self.V1[jo, tb * 512 + ts * 128: tb * 512 + (ts + 1) * 128, :],
                                        tv_sb[ts % 2]).then_inc(sst, 16)
                            tv_st[ts % 2] = vst
                        self._p5_vt_free = self.cur("p5_mm")
            x2_free[p] = self.cur("p5_mm")
            self.emit_tr(2)
        self.p5_done = self.phase_done("p5")
        self.last_done = self.p5_done
        st.close()

    def p6(self):
        self.dma_entry_fence()
        c = self.cfg
        t, v, sc, g, s = (self.q["tensor"], self.q["vector"], self.q["scalar"],
                          self.q["gpsimd"], self.q["sync"])
        NE, NT, TB, NG = c.NE, c.NT, c.TB, c.NG
        PER = c.NH // NG
        st = ExitStack()
        kt_sb = self.sbt(st, "p6_kt", [128, c.T], BF16)
        vv_sb = self.sbt(st, "p6_vv", [128, NT * DH], BF16)
        qt_sb = self.sbt(st, "p6_qt", [128, c.T], BF16)
        sq_sb = self.sbt(st, "p6_sq", [128, c.T], BF16)
        p_sb = [self.sbt(st, f"p6_p{i}", [128, NT * 512], BF16) for i in range(2)]
        o_sb = [self.sbt(st, f"p6_o{i}", [128, 512], BF16) for i in range(2)]
        r_sb = [self.sbt(st, f"p6_r{i}", [1, 512], F32) for i in range(2)]
        lnt_sb = [self.sbt(st, f"p6_lnt{i}", [1, 512], F32) for i in range(2)]
        br_sb = self.sbt(st, "p6_br", [128, 512], F32)
        mx_sb = self.sbt(st, "p6_mx", [1, 16], F32)
        biasb = self.sbt(st, "p6_bias", [128, 1], F32)

        self.mask_sb = self.sbt(st, "mask_sb", [128, 4 * 512], BF16)
        if self.cur("ld_mask"):
            g.wait_ge(self.sem("ld_mask"), self.cur("ld_mask"))
        smk, vmk = self.bump("ld_mask", 16)
        g.dma_start(self.mask_sb, self.c_masks).then_inc(smk, 16)
        sp_pair = [self.pst(st, f"p6_s{i}", [128, 1024]) for i in range(2)]
        op_ps = [self.pst(st, f"p6_op{i}", [128, 512]) for i in range(2)]
        aux_ps = self.pst(st, "p6_aux", [128, 512])
        sum_ps = [aux_ps[0:1, :], aux_ps[32:33, :]]
        nrm_ps = aux_ps[64:65, :]
        br_ps = self.pst(st, "p6_brp", [128, 512])
        nrm2_ps = br_ps[0:1, :]

        nchunk = c.T // 512
        o_st = [0, 0]
        rv_used = [0, 0]
        nrm_rd = [0, 0]
        pvmm = [0, 0]
        self.fence(s, self.p5_done)
        self.fence(g, self.p5_done)
        v.wait_ge(smk, vmk)
        for gg in range(NG):
            mmg_prev = self.cur("p6_mm")
            acg_prev = self.cur("p6_ac")
            g.wait_ge(self.sem("p6_mm"), mmg_prev)
            skh, vkh = self.bump("p6_ldk", 32)
            g.dma_start(kt_sb, self.KT1[gg * 128:(gg + 1) * 128, :]).then_inc(skh, 16)
            g.dma_start(vv_sb, self.V1[gg].rearrange("(a p) c -> p a c", p=128)).then_inc(skh, 16)
            sc.wait_ge(skh, vkh)
            if gg > 0:
                sc.wait_ge(self.sem("p6_mm"), mmg_prev)
            ska, vka = self.bump("p6_ac")
            sc.activation(sq_sb, kt_sb, AF.Square).then_inc(ska, 1)
            t.wait_ge(self.sem("p6_ac"), acg_prev)
            t.wait_ge(self.sem("p6_vc"), self.cur("p6_vc"))
            for ci in range(nchunk):
                t.wait_ge(ska, vka)
                bk = ci % 2
                nb = nrm_ps if bk == 0 else nrm2_ps
                if nrm_rd[bk]:
                    t.wait_ge(self.sem("p6_nrx"), nrm_rd[bk])
                mi = t.matmul(nb, self.ones_sb, sq_sb[:, ci * 512:(ci + 1) * 512],
                              start=True, stop=True)
                smm, vmm = self.bump("p6_mm")
                mi.then_inc(smm, 1)
                v.wait_ge(smm, vmm)
                sx, vx = self.bump("p6_nrx")
                v.reduce_max(mx_sb[:, 8 + ci:8 + ci + 1], nb, axis=AX).then_inc(sx, 1)
                nrm_rd[bk] = vx
            v.wait_ge(self.sem("p6_nrx"), self.cur("p6_nrx"))
            sv2, vv2 = self.bump("p6_vc")
            v.reduce_max(mx_sb[:, 15:16], mx_sb[:, 8:8 + nchunk], axis=AX).then_inc(sv2, 1)

            for hh in range(PER):
                h = gg * PER + hh
                mm_prev = self.cur("p6_mm")
                ac_prev = self.cur("p6_ac")
                g.wait_ge(self.sem("p6_mm"), mm_prev)
                if self.cur("p6_ldq"):
                    g.wait_ge(self.sem("p6_ldq"), self.cur("p6_ldq"))
                sqh, vqh = self.bump("p6_ldq", 16)
                g.dma_start(qt_sb, self.QT1[h * 128:(h + 1) * 128, :]).then_inc(sqh, 16)
                sc.wait_ge(sqh, vqh)
                sc.wait_ge(self.sem("p6_mm"), mm_prev)
                sa, va = self.bump("p6_ac")
                sc.activation(sq_sb, qt_sb, AF.Square).then_inc(sa, 1)
                t.wait_ge(self.sem("p6_vc"), self.cur("p6_vc"))
                for ci in range(nchunk):
                    t.wait_ge(sa, va)
                    bk = ci % 2
                    nb = nrm_ps if bk == 0 else nrm2_ps
                    if nrm_rd[bk]:
                        t.wait_ge(self.sem("p6_nrx"), nrm_rd[bk])
                    mi = t.matmul(nb, self.ones_sb, sq_sb[:, ci * 512:(ci + 1) * 512],
                                  start=True, stop=True)
                    smm, vmm = self.bump("p6_mm")
                    mi.then_inc(smm, 1)
                    v.wait_ge(smm, vmm)
                    sx, vx = self.bump("p6_nrx")
                    v.reduce_max(mx_sb[:, ci:ci + 1], nb, axis=AX).then_inc(sx, 1)
                    nrm_rd[bk] = vx
                v.wait_ge(self.sem("p6_nrx"), self.cur("p6_nrx"))
                sv1, vv1 = self.bump("p6_vc")
                v.reduce_max(mx_sb[:, 14:15], mx_sb[:, 0:nchunk], axis=AX).then_inc(sv1, 1)
                v.wait_ge(sv1, vv1)
                sv3, vv3 = self.bump("p6_vc")
                v.tensor_tensor(mx_sb[:, 13:14], mx_sb[:, 14:15], mx_sb[:, 15:16],
                                ALU.mult).then_inc(sv3, 1)
                sc.wait_ge(sv3, vv3)
                sa4, va4 = self.bump("p6_ac")
                sc.activation(mx_sb[:, 12:13], mx_sb[:, 13:14], AF.Sqrt).then_inc(sa4, 1)
                sc.wait_ge(sa4, va4)
                sa5, va5 = self.bump("p6_ac")
                sc.mul(mx_sb[:, 11:12], mx_sb[:, 12:13], -1.01 * ISCALE).then_inc(sa5, 1)
                t.wait_ge(sa5, va5)
                mi = t.matmul(br_ps[:, 0:1], self.onesr_sb, mx_sb[:, 11:12], start=True, stop=True)
                smm, vmm = self.bump("p6_mm")
                mi.then_inc(smm, 1)
                v.wait_ge(smm, vmm)
                if gg + hh > 0:
                    v.wait_ge(self.sem("p6_ac"), ac_prev)
                sbb, vbb = self.bump("p6_bv")
                v.tensor_copy(biasb, br_ps[:, 0:1]).then_inc(sbb, 1)
                sc.wait_ge(sbb, vbb)

                pend = None
                prev = None
                br_ev = [0]

                def fin6(qb_, rv_, h=h):
                    pqb_ = qb_ % 2
                    t.wait_ge(self.sem("p6_ac"), rv_)
                    if br_ev[0]:
                        t.wait_ge(self.sem("p6_vc"), br_ev[0])
                    mi_ = t.matmul(br_ps, self.onesr_sb, r_sb[pqb_], start=True, stop=True)
                    smm_, vmm_ = self.bump("p6_mm")
                    mi_.then_inc(smm_, 1)
                    v.wait_ge(smm_, vmm_)
                    for pp_ in range(2):
                        if self.cur(f"p6_pv{pp_}"):
                            v.wait_ge(self.sem(f"p6_pv{pp_}"), self.cur(f"p6_pv{pp_}"))
                    se_, ve_ = self.bump("p6_vc")
                    v.tensor_copy(br_sb, br_ps).then_inc(se_, 1)
                    br_ev[0] = ve_
                    v.wait_ge(se_, ve_)
                    if o_st[pqb_]:
                        v.wait_ge(self.sem(f"p6_st{pqb_}"), o_st[pqb_])
                    so_, vo_ = self.bump(f"p6_pv{pqb_}")
                    v.tensor_tensor(o_sb[pqb_], op_ps[pqb_], br_sb, ALU.mult).then_inc(so_, 1)
                    s.wait_ge(so_, vo_)
                    if o_st[pqb_]:
                        s.wait_ge(self.sem(f"p6_st{pqb_}"), o_st[pqb_])
                    sst_, vst_ = self.bump(f"p6_st{pqb_}", 16)
                    s.dma_start(self.OT0[h * 128:(h + 1) * 128, qb_ * 512:(qb_ + 1) * 512],
                                o_sb[pqb_]).then_inc(sst_, 16)
                    o_st[pqb_] = vst_

                def pv_tile6(qb_, eb_, mb_, ntv_, tt_, first, last):
                    pq_ = qb_ % 2
                    if tt_ >= ntv_ - 4:
                        di_ = (tt_ - (ntv_ - 4)) // 2
                        t.wait_ge(self.sem("p6_msk"), mb_ + di_ + 1)
                    if first:
                        if self.cur(f"p6_pv{pq_}"):
                            t.wait_ge(self.sem(f"p6_pv{pq_}"), self.cur(f"p6_pv{pq_}"))
                        if rv_used[pq_]:
                            t.wait_ge(self.sem("p6_ac"), rv_used[pq_])
                    t.matmul(op_ps[pq_], vv_sb[:, tt_ * DH:(tt_ + 1) * DH],
                             p_sb[pq_][:, tt_ * 512:(tt_ + 1) * 512],
                             start=first, stop=last, skip_group_check=True)
                    mi_ = t.matmul(sum_ps[pq_], self.ones_sb,
                                   p_sb[pq_][:, tt_ * 512:(tt_ + 1) * 512],
                                   start=first, stop=last, skip_group_check=True)
                    if last:
                        smm_, vmm_ = self.bump("p6_mm")
                        mi_.then_inc(smm_, 1)
                        return vmm_
                    return None

                def b_fin6(qb_, vmm_):
                    nonlocal pend
                    pq_ = qb_ % 2
                    sc.wait_ge(self.sem("p6_mm"), vmm_)
                    sr, vr = self.bump("p6_ac")
                    sc.activation(lnt_sb[pq_], sum_ps[pq_], AF.Ln).then_inc(sr, 1)
                    sc.wait_ge(sr, vr)
                    sr2, vr2 = self.bump("p6_ac")
                    sc.activation(r_sb[pq_], lnt_sb[pq_], AF.Exp, scale=-1.0).then_inc(sr2, 1)
                    rv_used[pq_] = vr2
                    pvmm[pq_] = vmm_
                    if pend is not None:
                        fin6(*pend)
                    pend = (qb_, vr2)

                for qb in range(TB):
                    pqb = qb % 2
                    nt_valid = qb * 4 + 4
                    npr = nt_valid // 2
                    if pvmm[pqb]:
                        sc.wait_ge(self.sem("p6_mm"), pvmm[pqb])
                    exp_base = self.cur("p6_ac")
                    msk_base = self.cur("p6_msk")
                    bvmm = None
                    for pr in range(npr):
                        tt0 = 2 * pr
                        pj = sp_pair[pr % 2]
                        if pr >= 2:
                            t.wait_ge(self.sem("p6_ac"), exp_base + pr - 1)
                        elif pr == 0 and prev is not None:
                            t.wait_ge(self.sem("p6_ac"), prev[1] + prev[3] // 2)
                        mi = t.matmul(pj[:, 0:512], kt_sb[:, tt0 * 128:(tt0 + 1) * 128],
                                      qt_sb[:, qb * 512:(qb + 1) * 512], start=True, stop=True)
                        mi = t.matmul(pj[:, 512:1024], kt_sb[:, (tt0 + 1) * 128:(tt0 + 2) * 128],
                                      qt_sb[:, qb * 512:(qb + 1) * 512], start=True, stop=True)
                        smm, vmm = self.bump("p6_mm")
                        mi.then_inc(smm, 1)
                        sc.wait_ge(smm, vmm)
                        sae, _ = self.bump("p6_ac")
                        sc.activation(p_sb[pqb][:, tt0 * 512:(tt0 + 2) * 512], pj, AF.Exp,
                                      bias=biasb, scale=ISCALE).then_inc(sae, 1)
                        if tt0 >= nt_valid - 4:
                            di = tt0 - (nt_valid - 4)
                            v.wait_ge(self.sem("p6_ac"), exp_base + pr + 1)
                            smk, _ = self.bump("p6_msk")
                            v.tensor_tensor(p_sb[pqb][:, tt0 * 512:(tt0 + 2) * 512],
                                            p_sb[pqb][:, tt0 * 512:(tt0 + 2) * 512],
                                            self.mask_sb[:, di * 512:(di + 2) * 512],
                                            ALU.mult).then_inc(smk, 1)
                        if prev is not None and tt0 < prev[3]:
                            for d_ in (0, 1):
                                ttp = tt0 + d_
                                r_ = pv_tile6(prev[0], prev[1], prev[2], prev[3],
                                              ttp, ttp == 0, ttp == prev[3] - 1)
                                if r_ is not None:
                                    bvmm = r_
                        self.emit_build_ops(2)
                    if prev is not None:
                        b_fin6(prev[0], bvmm)
                    prev = (qb, exp_base, msk_base, nt_valid)
                    self.emit_tr(1)
                qb_, eb_, mb_, ntv_ = prev
                bvmm = None
                for tt in range(ntv_):
                    if tt % 2 == 0 and tt < ntv_ - 4:
                        t.wait_ge(self.sem("p6_ac"), eb_ + tt // 2 + 1)
                    r_ = pv_tile6(qb_, eb_, mb_, ntv_, tt, tt == 0, tt == ntv_ - 1)
                    if r_ is not None:
                        bvmm = r_
                b_fin6(qb_, bvmm)
                if pend is not None:
                    fin6(*pend)
                    pend = None
                prev = None
        self.p6_done = self.phase_done("p6")
        self.last_done = self.p6_done
        st.close()

    def p6b(self):
        self.dma_entry_fence()
        c = self.cfg
        t, v, g, s = self.q["tensor"], self.q["vector"], self.q["gpsimd"], self.q["sync"]
        NE, TB = c.NE, c.TB
        st = ExitStack()
        ot = [self.sbt(st, f"p6b_ot{i}", [128, NE * 512], BF16) for i in range(2)]
        x2t = [self.sbt(st, f"p6b_x2{i}", [128, NE * 512], BF16) for i in range(2)]
        sum_ps = self.pst(st, "p6b_sum", [1, 512])
        sq_ps = self.pst(st, "p6b_sq", [1, 512])
        mb_ps = self.pst(st, "p6b_mbp", [128, 512])
        rb_ps = self.pst(st, "p6b_rbp", [128, 512])
        lnr = self.make_ln_res(st, "p6b", sum_ps, sq_ps, mb_ps, rb_ps, BF16, depth=2)

        self.fence(s, self.p6_done)
        self.fence(g, self.p6_done)
        buf_free = [0, 0]
        for tb in range(TB):
            p = tb % 2
            if buf_free[p]:
                z0, z1 = buf_free[p]
                g.wait_ge(self.sem("p6b_zadd0"), z0)
                g.wait_ge(self.sem("p6b_zadd1"), z1)
            if self.cur(f"p6b_ldo{p}"):
                g.wait_ge(self.sem(f"p6b_ldo{p}"), self.cur(f"p6b_ldo{p}"))
            slo, vlo = self.bump(f"p6b_ldo{p}", 16 * 2 * NE)
            for e_i in range(NE):
                g.dma_start(ot[p][:, e_i * 512:(e_i + 1) * 512],
                            self.OT0[e_i * 128:(e_i + 1) * 128,
                                     tb * 512:(tb + 1) * 512]).then_inc(slo, 16)
                g.dma_start(x2t[p][:, e_i * 512:(e_i + 1) * 512],
                            self.X2[e_i * 128:(e_i + 1) * 128,
                                    tb * 512:(tb + 1) * 512]).then_inc(slo, 16)
            v.wait_ge(slo, vlo)
            lnr["cur_par"] = tb % 2
            for j in range(NE):
                self.ln_z("p6b", lnr, j, ot[p][:, j * 512:(j + 1) * 512], [],
                          x2t[p][:, j * 512:(j + 1) * 512])
                self.emit_build_ops(2)
            if tb > 0:
                lnr["norm_par"] = (tb - 1) % 2
                self.ln_norm("p6b", lnr, tb - 1, self.X3)
            self.ln_stats("p6b", lnr, tb)
            self.emit_tr(2)
            buf_free[p] = (self.cur("p6b_zadd0"), self.cur("p6b_zadd1"))
        lnr["norm_par"] = (TB - 1) % 2
        self.ln_norm("p6b", lnr, TB - 1, self.X3)
        self.p6b_done = self.phase_done("p6b")
        self.last_done = self.p6b_done
        st.close()

    def p7(self):
        g, s = self.q["gpsimd"], self.q["sync"]
        self.emit_build(len(self.bw_jobs))
        s.wait_ge(self.sem("bw_sta"), self.bw_sta)
        g.wait_ge(self.sem("bw_sta"), self.bw_sta)
        self.p7_done = [("bw_sta", self.cur("bw_sta"))]
        self.last_done = self.p7_done

    def p7b(self):
        g = self.q["gpsimd"]
        self.emit_tr(len(self.tr_jobs), margin=0)
        self._tr_store()
        self.tr_stack.close()
        self.tr_stack = None
        self.bw_stack.close()
        if self.tr_st[0]:
            g.wait_ge(self.sem("tr_st0"), self.tr_st[0])
        if self.tr_st[1]:
            g.wait_ge(self.sem("tr_st1"), self.tr_st[1])
        self.p7b_done = [("tr_st0", self.tr_st[0]), ("tr_st1", self.tr_st[1])]
        self.last_done = self.p7b_done

    def p8(self):
        self.dma_entry_fence()
        c = self.cfg
        t, v, sc, g, s = (self.q["tensor"], self.q["vector"], self.q["scalar"],
                          self.q["gpsimd"], self.q["sync"])
        NE, TB, NI = c.NE, c.TB, c.NI
        st = ExitStack()
        x3t = [self.sbt(st, f"p8_x3{i}", [128, NE * 512], BF16) for i in range(2)]
        gst = [self.sbt(st, f"p8_gs{i}", [128, NE * 128], BF16) for i in range(2)]
        ust = [self.sbt(st, f"p8_us{i}", [128, NE * 128], BF16) for i in range(2)]
        dst_b = [self.sbt(st, f"p8_ds{i}", [128, NI * 256], BF16) for i in range(2)]
        h_sb = self.sbt(st, "p8_h", [128, NI * 512], BF16)
        sil_sb = self.sbt(st, "p8_sil", [128, 512], BF16)

        g_ps = self.pst(st, "p8_g", [128, 512])
        u_ps = self.pst(st, "p8_u", [128, 512])
        d_ps = [self.pst(st, f"p8_d{i}", [128, 512]) for i in range(4)]
        sum_ps = self.pst(st, "p8_sum", [1, 512])
        sq_ps = self.pst(st, "p8_sq", [1, 512])
        lnr = self.make_ln_res(st, "p8", sum_ps, sq_ps, g_ps, u_ps, F32)

        self.fence(s, self.p6b_done)
        self.fence(s, self.p7b_done)
        self.fence(g, self.p6b_done)
        self.fence(g, self.p7b_done)

        x3_free = [0, 0]
        gs_free = [0, 0]
        ds_free = [0, 0]
        for tb in range(TB):
            p = tb % 2
            if x3_free[p]:
                g.wait_ge(self.sem("p8_mm"), x3_free[p][0])
                for bi in range(4):
                    if x3_free[p][1 + bi]:
                        g.wait_ge(self.sem(f"p8_zadd{bi}"), x3_free[p][1 + bi])
            if self.cur(f"p8_ldo{p}"):
                g.wait_ge(self.sem(f"p8_ldo{p}"), self.cur(f"p8_ldo{p}"))
            slx, vlx = self.bump(f"p8_ldo{p}", 16 * NE)
            for e_i in range(NE):
                g.dma_start(x3t[p][:, e_i * 512:(e_i + 1) * 512],
                            self.X3[e_i * 128:(e_i + 1) * 128,
                                    tb * 512:(tb + 1) * 512]).then_inc(slx, 16)
            t.wait_ge(slx, vlx)
            hmm_gate = self.cur("p8_mm") if tb > 0 else 0
            for it in range(NI):
                pw = it % 2
                if gs_free[pw]:
                    g.wait_ge(self.sem("p8_mm"), gs_free[pw])
                slw, vlw = self.bump(f"p8_ldw{pw}", 32)
                g.dma_start(gst[pw],
                            self.GWT[:, it * 128:(it + 1) * 128]
                                .rearrange("(a p) c -> p a c", p=128)).then_inc(slw, 16)
                g.dma_start(ust[pw],
                            self.UWT[:, it * 128:(it + 1) * 128]
                                .rearrange("(a p) c -> p a c", p=128)).then_inc(slw, 16)
                t.wait_ge(slw, vlw)
                need = self.cur("p8_sil")
                if need:
                    t.wait_ge(self.sem("p8_sil"), need)
                needh = self.cur("p8_hw")
                if needh:
                    t.wait_ge(self.sem("p8_hw"), needh)
                if it == 0 and lnr.get("free_wait"):
                    t.wait_ge(*lnr["free_wait"])
                for e_i in range(NE):
                    mi = t.matmul(g_ps, gst[pw][:, e_i * 128:(e_i + 1) * 128],
                                  x3t[p][:, e_i * 512:(e_i + 1) * 512],
                                  start=(e_i == 0), stop=(e_i == NE - 1), skip_group_check=True)
                for e_i in range(NE):
                    mi2 = t.matmul(u_ps, ust[pw][:, e_i * 128:(e_i + 1) * 128],
                                   x3t[p][:, e_i * 512:(e_i + 1) * 512],
                                   start=(e_i == 0), stop=(e_i == NE - 1), skip_group_check=True)
                smm, vmm = self.bump("p8_mm")
                mi2.then_inc(smm, 1)
                gs_free[pw] = vmm
                sc.wait_ge(smm, vmm)
                if self.cur("p8_hw"):
                    sc.wait_ge(self.sem("p8_hw"), self.cur("p8_hw"))
                ssl, vsl = self.bump("p8_sil")
                sc.activation(sil_sb, g_ps, AF.Silu).then_inc(ssl, 1)
                v.wait_ge(ssl, vsl)
                if tb > 0 and it == 0:
                    v.wait_ge(self.sem("p8_mm"), hmm_gate)
                shw, _ = self.bump("p8_hw")
                v.tensor_tensor(h_sb[:, it * 512:(it + 1) * 512], sil_sb, u_ps,
                                ALU.mult).then_inc(shw, 1)
            hw_all = self.cur("p8_hw")
            for og in range(NE // 2):
                pw = og % 2
                if ds_free[pw]:
                    g.wait_ge(self.sem("p8_mm"), ds_free[pw])
                if self.cur(f"p8_ldd{pw}"):
                    g.wait_ge(self.sem(f"p8_ldd{pw}"), self.cur(f"p8_ldd{pw}"))
                slw, vlw = self.bump(f"p8_ldd{pw}", 16)
                g.dma_start(dst_b[pw],
                            self.DWT[:, og * 256:(og + 1) * 256]
                                .rearrange("(a p) c -> p a c", p=128)).then_inc(slw, 16)
                t.wait_ge(slw, vlw)
                if og == 0:
                    t.wait_ge(self.sem("p8_hw"), hw_all)
                for oi in range(2):
                    j = og * 2 + oi
                    bk = j % 4
                    need = self.cur(f"p8_zadd{bk}")
                    if need:
                        t.wait_ge(self.sem(f"p8_zadd{bk}"), need)
                for it in range(NI):
                    for oi in range(2):
                        mi = t.matmul(d_ps[(og * 2 + oi) % 4],
                                      dst_b[pw][:, it * 256 + oi * 128: it * 256 + (oi + 1) * 128],
                                      h_sb[:, it * 512:(it + 1) * 512],
                                      start=(it == 0), stop=(it == NI - 1), skip_group_check=True)
                smm, vmm = self.bump("p8_mm")
                mi.then_inc(smm, 1)
                ds_free[pw] = vmm
                for oi in range(2):
                    j = og * 2 + oi
                    self.ln_z("p8", lnr, j, d_ps[j % 4], [(smm, vmm)],
                              x3t[p][:, j * 512:(j + 1) * 512], bank=j % 4)
            self.ln_finish("p8", lnr, tb, self.OUT_T)
            x3_free[p] = (self.cur("p8_mm"), self.cur("p8_zadd0"), self.cur("p8_zadd1"),
                          self.cur("p8_zadd2"), self.cur("p8_zadd3"))
        self.p8_done = self.phase_done("p8")
        self.last_done = self.p8_done
        st.close()

    def finalize(self):
        s = self.q["sync"]
        self.fence(s, self.last_done)
        with self.nc.Block() as block:
            block.tensor(lambda e: self.q["tensor"].replay(e))
            block.vector(lambda e: self.q["vector"].replay(e))
            block.scalar(lambda e: self.q["scalar"].replay(e))
            block.gpsimd(lambda e: self.q["gpsimd"].replay(e))
            block.sync(lambda e: self.q["sync"].replay(e))
        return self.nc


def build(cfg: Cfg, debug_outputs=(), phases=None):
    b = Builder(cfg, debug_outputs)
    b.declare_io()
    b.declare_consts()
    allp = ["p1", "p2", "p3", "p4", "p5", "p6", "p6b", "p7", "p7b", "p8"]
    if phases is None:
        phases = allp
    b.build_init()
    for ph in phases:
        getattr(b, ph)()
    return b.finalize(), b


def make_core_inputs(params, b):
    C = np.ascontiguousarray
    return {
        "x": params["x"],
        "wqT": C(params["wq"].T),
        "wdkvT": C(params["wdkv"].T),
        "wupkT": C(params["wupk"].transpose(0, 2, 1)),
        "wupvT": C(params["wupv"].transpose(0, 2, 1)),
        "woT": C(params["wo"].T),
        "wffT": C(params["wff"].T),
        "gatewT": C(params["gatew"].T),
        "awT": C(params["aw"].reshape(128, -1).T),
        "bwT": C(params["bw"].transpose(0, 2, 1).reshape(128, -1)),
        "wq1": params["wq1"],
        "bq1": params["bq1"],
        "wk1": params["wk1"],
        "bk1": params["bk1"],
        "wv1": params["wv1"],
        "bv1": params["bv1"],
        "lut": params["lut"].reshape(1, 16),
        "gidx": params["gidx"],
        "uidx": params["uidx"],
        "didx": params["didx"],
    }


def kernel(L0_attn_x, L0_attn_W_q, L0_attn_W_down_kv, L0_attn_W_up_k, L0_attn_W_up_v,
           L0_attn_W_o, L0_ffn_weight, L0_ffn_gate_w, L0_ffn_A_w, L0_ffn_B_w,
           L1_attn_Wq, L1_attn_bq, L1_attn_Wk, L1_attn_bk, L1_attn_Wv, L1_attn_bv,
           L1_ffn_lut, L1_ffn_gate_idx, L1_ffn_up_idx, L1_ffn_down_idx):
    x = np.asarray(L0_attn_x)
    B = x.shape[0]
    cfg = Cfg(T=x.shape[1], E=x.shape[2])
    nc, _ = build(cfg)
    shared = {
        "wq": np.asarray(L0_attn_W_q), "wdkv": np.asarray(L0_attn_W_down_kv),
        "wupk": np.asarray(L0_attn_W_up_k), "wupv": np.asarray(L0_attn_W_up_v),
        "wo": np.asarray(L0_attn_W_o), "wff": np.asarray(L0_ffn_weight),
        "gatew": np.asarray(L0_ffn_gate_w), "aw": np.asarray(L0_ffn_A_w),
        "bw": np.asarray(L0_ffn_B_w),
        "wq1": np.asarray(L1_attn_Wq), "bq1": np.asarray(L1_attn_bq),
        "wk1": np.asarray(L1_attn_Wk), "bk1": np.asarray(L1_attn_bk),
        "wv1": np.asarray(L1_attn_Wv), "bv1": np.asarray(L1_attn_bv),
        "lut": np.asarray(L1_ffn_lut), "gidx": np.asarray(L1_ffn_gate_idx),
        "uidx": np.asarray(L1_ffn_up_idx), "didx": np.asarray(L1_ffn_down_idx),
    }
    base = make_core_inputs(dict(shared, x=x[0]), None)
    in_maps = []
    for bi in range(B):
        m = dict(base)
        m["x"] = np.ascontiguousarray(x[bi])
        in_maps.append(m)
    res = run_bass_kernel_spmd(nc, in_maps, core_ids=list(range(8)))
    outs = [np.ascontiguousarray(np.asarray(r["OUT_T"]).T) for r in res.results]
    return np.stack(outs, 0)


# version 34
# speedup vs baseline: 1.0283x; 1.0283x over previous
import math
from contextlib import ExitStack

import numpy as np
import ml_dtypes
import concourse.bass as bass
import concourse.mybir as mybir
from concourse.bass_utils import run_bass_kernel_spmd

F32 = mybir.dt.float32
BF16 = mybir.dt.bfloat16
I32 = mybir.dt.int32
AF = mybir.ActivationFunctionType
ALU = mybir.AluOpType
AX = mybir.AxisListType.X

ENGS = ("tensor", "vector", "scalar", "gpsimd", "sync")
DH = 128
NEXP = 8
RANK = 16
ISCALE = float(1.0 / math.sqrt(DH))
LNEPS = 1e-5


class Rec:
    def __init__(self):
        self.items = []

    def __getattr__(self, name):
        if name.startswith("_"):
            raise AttributeError(name)
        items = self.items
        def call(*a, **k):
            rec = [name, a, k, None]
            items.append(rec)
            class _G:
                def then_inc(self, sem, n):
                    rec[3] = (sem, n)
                    return self
            return _G()
        return call

    def replay(self, eng):
        for name, a, k, inc in self.items:
            r = getattr(eng, name)(*a, **k)
            if inc is not None:
                r.then_inc(*inc)


class Cfg:
    def __init__(self, T=4096, E=2048, NH=16, DL=512, NG=4, INTER=5504):
        assert E == NH * DH
        self.T, self.E, self.NH, self.DL, self.NG, self.INTER = T, E, NH, DL, NG, INTER
        self.NT = T // 128
        self.TB = T // 512
        self.NE = E // 128
        self.NL = DL // 128
        self.NI = INTER // 128
        assert T % 512 == 0 and DL % 128 == 0 and INTER % 128 == 0


class Builder:
    def __init__(self, cfg: Cfg, debug_outputs=()):
        self.cfg = cfg
        self.dbg = set(debug_outputs)
        self.nc = bass.Bass("TRN2", target_bir_lowering=False)
        self.q = {e: Rec() for e in ENGS}
        self.stack = ExitStack()
        self.sems = {}
        self.val = {}

    @staticmethod
    def _canon(name):
        import re as _re
        return _re.sub(r"^p\d+b?_", "", name)

    def sem(self, name):
        name = self._canon(name)
        if name not in self.sems:
            self.sems[name] = self.stack.enter_context(self.nc.semaphore(name))
            self.val[name] = 0
        return self.sems[name]

    def bump(self, name, n=1):
        s = self.sem(name)
        name = self._canon(name)
        self.val[name] += n
        return s, self.val[name]

    def cur(self, name):
        self.sem(name)
        return self.val[self._canon(name)]

    def phase_done(self, ph):
        return [(n, self.val[n]) for n in list(self.val)
                if n.startswith("st") or n.startswith("sv")]

    def fence(self, rec, done):
        for (n, v) in done:
            rec.wait_ge(self.sem(n), v)

    def dma_entry_fence(self, exclude=()):
        for ename in ("gpsimd", "sync", "scalar"):
            rec = self.q[ename]
            for n, v in list(self.val.items()):
                if v and not any(n.startswith(p) for p in exclude):
                    rec.wait_ge(self.sems[n], v)

    def dram(self, name, shape, dtype, kind="Internal"):
        if name in self.dbg and kind == "Internal":
            kind = "ExternalOutput"
        return self.nc.dram_tensor(name, list(shape), dtype, kind=kind).ap()

    def sb(self, name, shape, dtype, **kw):
        return self.nc.alloc_sbuf_tensor(name, list(shape), dtype, **kw).ap()

    def sbt(self, st, name, shape, dtype, **kw):
        return st.enter_context(self.nc.sbuf_tensor(name, list(shape), dtype, **kw)).ap()

    def pst(self, st, name, shape, dtype=F32):
        return st.enter_context(self.nc.psum_tensor(name, list(shape), dtype)).ap()

    def declare_io(self):
        c, d = self.cfg, self.dram
        self.x = d("x", [c.T, c.E], F32, "ExternalInput")
        self.wqT = d("wqT", [c.E, c.E], F32, "ExternalInput")
        self.wdkvT = d("wdkvT", [c.E, c.DL], F32, "ExternalInput")
        self.wupkT = d("wupkT", [c.NH, c.DL, DH], F32, "ExternalInput")
        self.wupvT = d("wupvT", [c.NH, c.DL, DH], F32, "ExternalInput")
        self.woT = d("woT", [c.E, c.E], F32, "ExternalInput")
        self.wffT = d("wffT", [c.E, c.E], F32, "ExternalInput")
        self.gatewT = d("gatewT", [c.E, NEXP], F32, "ExternalInput")
        self.awT = d("awT", [c.E, 128], F32, "ExternalInput")
        self.bwT = d("bwT", [128, c.E], F32, "ExternalInput")
        self.wq1 = d("wq1", [c.NH, c.E, DH], F32, "ExternalInput")
        self.bq1 = d("bq1", [c.NH, DH], F32, "ExternalInput")
        self.wk1 = d("wk1", [c.NG, c.E, DH], F32, "ExternalInput")
        self.bk1 = d("bk1", [c.NG, DH], F32, "ExternalInput")
        self.wv1 = d("wv1", [c.NG, c.E, DH], F32, "ExternalInput")
        self.bv1 = d("bv1", [c.NG, DH], F32, "ExternalInput")
        self.lut = d("lut", [1, 16], F32, "ExternalInput")
        self.gidx = d("gidx", [c.INTER, c.E], I32, "ExternalInput")
        self.uidx = d("uidx", [c.INTER, c.E], I32, "ExternalInput")
        self.didx = d("didx", [c.E, c.INTER], I32, "ExternalInput")

        self.OUT_T = d("OUT_T", [c.E, c.T], F32, "ExternalOutput")

        self.xb = d("xb", [c.T, c.E], BF16)
        self.QT0 = d("QT0", [c.E, c.T], BF16)
        self.CT0 = d("CT0", [c.DL, c.T], BF16)
        self.OT0 = d("OT0", [c.E, c.T], BF16)
        self.X1 = d("X1", [c.E, c.T], BF16)
        self.X2 = d("X2", [c.E, c.T], BF16)
        self.X3 = d("X3", [c.E, c.T], BF16)
        self.QT1 = d("QT1", [c.E, c.T], BF16)
        self.KT1 = d("KT1", [c.NG * DH, c.T], BF16)
        self.V1 = d("V1", [c.NG, c.T, DH], BF16)
        self.GWB = d("GWB", [c.INTER, c.E], BF16)
        self.UWB = d("UWB", [c.INTER, c.E], BF16)
        self.DWB = d("DWB", [c.E, c.INTER], BF16)
        self.GWT = d("GWT", [c.E, c.INTER], BF16)
        self.UWT = d("UWT", [c.E, c.INTER], BF16)
        self.DWT = d("DWT", [c.INTER, c.E], BF16)

    def declare_consts(self):
        nc = self.nc
        c_ones = nc.inline_tensor(ml_dtypes.bfloat16(np.ones((128, 1), np.float32)), "c_ones").ap()
        c_onesr = nc.inline_tensor(np.ones((1, 128), np.float32), "c_onesr").ap()
        c_ident = nc.inline_tensor(ml_dtypes.bfloat16(np.eye(128, dtype=np.float32)), "c_ident").ap()
        e16 = np.zeros((NEXP, 128), np.float32)
        for e in range(NEXP):
            e16[e, e * RANK:(e + 1) * RANK] = 16.0
        c_e16 = nc.inline_tensor(ml_dtypes.bfloat16(e16), "c_e16").ap()
        masks = []
        for di in range(4):
            m = (np.arange(512)[None, :] >= (np.arange(128)[:, None] + di * 128)).astype(np.float32)
            masks.append(m)
        c_masks = nc.inline_tensor(ml_dtypes.bfloat16(np.concatenate(masks, 1)), "c_masks").ap()

        self.ones_sb = self.sb("ones_sb", [128, 1], BF16)
        self.onesr_sb = self.sb("onesr_sb", [1, 128], F32)
        self.id_sb = self.sb("id_sb", [128, 128], BF16)
        self.e16_sb = self.sb("e16_sb", [NEXP, 128], BF16)
        self.lutb_sb = self.sb("lutb_sb", [128, 16], F32)
        self.c_masks_dram = None
        self.lut_sb = self.sb("lut_sb", [1, 16], F32)

        g = self.q["gpsimd"]
        self.c_masks = c_masks
        s5, v5 = self.bump("ld_const", 5 * 16)
        g.dma_start(self.ones_sb, c_ones).then_inc(s5, 16)
        g.dma_start(self.onesr_sb, c_onesr).then_inc(s5, 16)
        g.dma_start(self.id_sb, c_ident).then_inc(s5, 16)
        g.dma_start(self.e16_sb, c_e16).then_inc(s5, 16)
        g.dma_start(self.lut_sb, self.lut).then_inc(s5, 16)
        self.const_ready = ("ld_const", v5)

    def make_ln_res(self, st, ph, sum_ps, sq_ps, mb_ps, rb_ps, out_dtype, depth=1):
        r = {
            "z_sb": [self.sbt(st, f"{ph}_z{i}", [128, self.cfg.NE * 512], F32)
                     for i in range(depth)],
            "zb": [self.sbt(st, f"{ph}_zb{i}", [128, 512], BF16) for i in range(2)],
            "zq": [self.sbt(st, f"{ph}_zq{i}", [128, 512], BF16) for i in range(2)],
            "outb": [self.sbt(st, f"{ph}_ob{i}", [128, 512], out_dtype) for i in range(2)],
            "t_sb": self.sbt(st, f"{ph}_t", [128, 512], F32),
            "rows": self.sbt(st, f"{ph}_rows", [128, 2048], F32),
            "mb_sb": [self.sbt(st, f"{ph}_mb{i}", [128, 512], F32) for i in range(depth)],
            "rb_sb": [self.sbt(st, f"{ph}_rb{i}", [128, 512], F32) for i in range(depth)],
            "sum_ps": sum_ps, "sq_ps": sq_ps, "mb_ps": mb_ps, "rb_ps": rb_ps,
            "stats_mm": [0, 0], "out_st": [0, 0], "depth": depth,
        }
        rows = r["rows"]
        r["m_sb"] = [rows[0:1, 0:512], rows[0:1, 512:1024]][:max(depth, 1)]
        r["r_sb"] = [rows[0:1, 1024:1536], rows[0:1, 1536:2048]][:max(depth, 1)]
        r["a_sb"] = rows[32:33, 0:512]
        r["b_sb"] = rows[32:33, 512:1024]
        r["c_sb"] = rows[64:65, 0:512]
        r["d_sb"] = rows[64:65, 512:1024]
        return r

    def _ln_emit_stats(self, ph, r):
        t = self.q["tensor"]
        NE = self.cfg.NE
        pend = r.get("pend_stats")
        if pend is None:
            return
        r["pend_stats"] = None
        j, szq, vzq = pend
        t.wait_ge(szq, vzq)
        t.matmul(r["sum_ps"], self.ones_sb, r["zb"][j % 2],
                 start=(j == 0), stop=(j == NE - 1), skip_group_check=True)
        mi = t.matmul(r["sq_ps"], self.ones_sb, r["zq"][j % 2],
                      start=(j == 0), stop=(j == NE - 1), skip_group_check=True)
        smm, vmm = self.bump(f"{ph}_mm")
        mi.then_inc(smm, 1)
        r["stats_mm"][j % 2] = vmm

    def ln_z(self, ph, r, j, ps_ap, ps_waits, res_ap, bank=None):
        cfg = self.cfg
        t, v = self.q["tensor"], self.q["vector"]
        NE = cfg.NE
        if bank is None:
            bank = j % 2
        self._ln_emit_stats(ph, r)
        for (sw, vw) in ps_waits:
            v.wait_ge(sw, vw)
        if r["stats_mm"][j % 2]:
            v.wait_ge(self.sem(f"{ph}_mm"), r["stats_mm"][j % 2])
        sc = self.q["scalar"]
        zp = r.get("cur_par", 0)
        zt = r["z_sb"][zp][:, j * 512:(j + 1) * 512]
        sz, vz = self.bump(f"{ph}_zadd{bank}")
        v.tensor_tensor(zt, ps_ap, res_ap, ALU.add).then_inc(sz, 1)
        sc.wait_ge(sz, vz)
        if r["stats_mm"][j % 2]:
            sc.wait_ge(self.sem(f"{ph}_mm"), r["stats_mm"][j % 2])
        szb, vzb = self.bump(f"{ph}_vz")
        sc.copy(r["zb"][j % 2], zt).then_inc(szb, 1)
        szq, vzq = self.bump(f"{ph}_vz")
        sc.activation(r["zq"][j % 2], zt, AF.Square).then_inc(szq, 1)
        r["pend_stats"] = (j, szq, vzq)

    def ln_stats(self, ph, r, tb):
        self._ln_emit_stats(ph, r)
        cfg = self.cfg
        t, v, sc = self.q["tensor"], self.q["vector"], self.q["scalar"]
        zp = r.get("cur_par", 0)
        smm, vmm = self.sem(f"{ph}_mm"), self.cur(f"{ph}_mm")
        v.wait_ge(smm, vmm)
        if r.get("mbmm"):
            v.wait_ge(self.sem(f"{ph}_mm"), r["mbmm"])
        s1, _ = self.bump(f"{ph}_vc")
        v.tensor_scalar(r["m_sb"][zp], r["sum_ps"], 1.0 / cfg.E, None, ALU.mult).then_inc(s1, 1)
        s2, v2 = self.bump(f"{ph}_vc")
        v.tensor_scalar(r["a_sb"], r["sq_ps"], 1.0 / cfg.E, None, ALU.mult).then_inc(s2, 1)
        v.wait_ge(s2, v2)
        s3, v3 = self.bump(f"{ph}_vc")
        v.tensor_tensor(r["b_sb"], r["m_sb"][zp], r["m_sb"][zp], ALU.mult).then_inc(s3, 1)
        v.wait_ge(s3, v3)
        s4, v4 = self.bump(f"{ph}_vc")
        v.scalar_tensor_tensor(r["c_sb"], r["a_sb"], LNEPS, r["b_sb"],
                               ALU.add, ALU.subtract).then_inc(s4, 1)
        sc.wait_ge(s4, v4)
        s5, v5 = self.bump(f"{ph}_ac")
        sc.activation(r["d_sb"], r["c_sb"], AF.Ln).then_inc(s5, 1)
        sc.wait_ge(s5, v5)
        s6, v6 = self.bump(f"{ph}_ac")
        sc.activation(r["r_sb"][zp], r["d_sb"], AF.Exp, scale=-0.5).then_inc(s6, 1)
        r.setdefault("s6_by", {})[zp] = (s6, v6)

    def ln_norm(self, ph, r, tb, dest):
        cfg = self.cfg
        t, v, s = self.q["tensor"], self.q["vector"], self.q["sync"]
        NE = cfg.NE
        zp = r.get("norm_par", 0)
        t.wait_ge(*r["s6_by"][zp])
        if r.get("free_wait"):
            t.wait_ge(*r["free_wait"])
        for w_ in r.get("extra_norm_wait", ()):
            t.wait_ge(*w_)
        mi = t.matmul(r["mb_ps"], self.onesr_sb, r["m_sb"][zp], start=True, stop=True)
        mi2 = t.matmul(r["rb_ps"], self.onesr_sb, r["r_sb"][zp], start=True, stop=True)
        smm2, vmm2 = self.bump(f"{ph}_mm")
        mi2.then_inc(smm2, 1)
        r["mbmm"] = vmm2
        v.wait_ge(smm2, vmm2)
        s7, _ = self.bump(f"{ph}_vc")
        v.tensor_copy(r["mb_sb"][zp], r["mb_ps"]).then_inc(s7, 1)
        s8, v8 = self.bump(f"{ph}_vc")
        v.tensor_copy(r["rb_sb"][zp], r["rb_ps"]).then_inc(s8, 1)
        v.wait_ge(s8, v8)
        r["free_wait"] = (self.sem(f"{ph}_vc"), v8)
        for j in range(NE):
            zt = r["z_sb"][zp][:, j * 512:(j + 1) * 512]
            ob = r["outb"][j % 2]
            if r["out_st"][j % 2]:
                v.wait_ge(self.sem(f"{ph}_st{j % 2}"), r["out_st"][j % 2])
            if j > 0:
                v.wait_ge(self.sem(f"{ph}_vc"), self.cur(f"{ph}_vc"))
            s9, v9 = self.bump(f"{ph}_vc")
            v.tensor_tensor(r["t_sb"], zt, r["mb_sb"][zp], ALU.subtract).then_inc(s9, 1)
            v.wait_ge(s9, v9)
            s10, v10 = self.bump(f"{ph}_vc")
            v.tensor_tensor(ob, r["t_sb"], r["rb_sb"][zp], ALU.mult).then_inc(s10, 1)
            s.wait_ge(s10, v10)
            if r["out_st"][j % 2]:
                s.wait_ge(self.sem(f"{ph}_st{j % 2}"), r["out_st"][j % 2])
            sst, vst = self.bump(f"{ph}_st{j % 2}", 16)
            s.dma_start(dest[j * 128:(j + 1) * 128, tb * 512:(tb + 1) * 512], ob).then_inc(sst, 16)
            r["out_st"][j % 2] = vst

    def ln_finish(self, ph, r, tb, dest):
        r["norm_par"] = r.get("cur_par", 0)
        self.ln_stats(ph, r, tb)
        self.ln_norm(ph, r, tb, dest)

    def p0(self):
        c = self.cfg
        g, s = self.q["gpsimd"], self.q["sync"]
        st = ExitStack()
        buf = [self.sbt(st, f"p0_buf{i}", [128, c.E], BF16) for i in range(4)]
        n = c.T // 128
        for i in range(n):
            p = i % 4
            if i >= 4:
                g.wait_ge(self.sem(f"p0_st{p}"), 16 * (i // 4))
            if self.cur(f"p0_ld{p}"):
                g.wait_ge(self.sem(f"p0_ld{p}"), self.cur(f"p0_ld{p}"))
            sl, vl = self.bump(f"p0_ld{p}", 16)
            g.dma_start(buf[p], self.x[i * 128:(i + 1) * 128, :]).then_inc(sl, 16)
            s.wait_ge(sl, vl)
            ss, _ = self.bump(f"p0_st{p}", 16)
            s.dma_start(self.xb[i * 128:(i + 1) * 128, :], buf[p]).then_inc(ss, 16)
        self.p0_done = self.phase_done("p0")
        self.last_done = self.p0_done
        st.close()

    def build_init(self):
        c = self.cfg
        t, v, g = self.q["tensor"], self.q["vector"], self.q["gpsimd"]
        self.bw_stack = ExitStack()
        self.bw_idx = [self.sbt(self.bw_stack, "bw_idx0", [128, 1024], BF16)]
        self.bw_acc = self.sbt(self.bw_stack, "bw_acc", [128, 1024], BF16)
        self.bw_tmp = self.sbt(self.bw_stack, "bw_tmp", [128, 1024], BF16)
        self.bw_jobs = []
        for (srct, dstt, nrow, w) in ((self.gidx, self.GWB, c.INTER, c.E),
                                      (self.uidx, self.UWB, c.INTER, c.E),
                                      (self.didx, self.DWB, c.E, c.INTER)):
            r0 = 0
            while r0 < nrow:
                rh = min(128, nrow - r0)
                for c0 in range(0, w, 1024):
                    cw = min(1024, w - c0)
                    self.bw_jobs.append((srct, dstt, r0, rh, c0, cw))
                r0 += rh
        self.bw_next = 0
        self.bw_cur = None
        self.bw_step = 0
        self.bw_dve = [0, 0]
        self.bw_sta = 0
        self.tr_stack = None
        self.tr_jobs = []
        bases = {id(self.GWB): (0, 2), id(self.UWB): (86, 2), id(self.DWB): (172, 6)}
        for (src, dstt, sr, sw) in ((self.GWB, self.GWT, c.INTER, c.E),
                                    (self.UWB, self.UWT, c.INTER, c.E),
                                    (self.DWB, self.DWT, c.E, c.INTER)):
            base, nch = bases[id(src)]
            chunks = []
            off = 0
            while off < sr:
                ck = min(1024, sr - off)
                chunks.append((off, ck))
                off += ck
            for band in range(sw // 128):
                c_idx = (band * 128) // 1024
                for (off, ck) in chunks:
                    last_rb = (off + ck) // 128 - 1
                    need = base + last_rb * nch + c_idx
                    self.tr_jobs.append(
                        (need,
                         src[off:off + ck, band * 128:(band + 1) * 128],
                         dstt[band * 128:(band + 1) * 128, off:off + ck], ck))
        self.tr_next = 0
        self.tr_st = [0, 0]
        self.tr_pend = None
        with self.nc.psum_tensor("bw_lb", [128, 16], F32) as lb_t:
            lb_ps = lb_t.ap()
            t.wait_ge(self.sem(self.const_ready[0]), self.const_ready[1])
            mi = t.matmul(lb_ps, self.onesr_sb, self.lut_sb, start=True, stop=True)
            smm, vmm = self.bump("bw_mm")
            mi.then_inc(smm, 1)
            v.wait_ge(smm, vmm)
            slb, vlb = self.bump("bw_vc")
            v.tensor_copy(self.lutb_sb, lb_ps).then_inc(slb, 1)
            self.bw_lut_ready = vlb

    def emit_build(self, n):
        self.emit_build_ops(31 * n)

    def emit_build_ops(self, nops):
        v, g, s = self.q["vector"], self.q["gpsimd"], self.q["sync"]
        while nops > 0:
            if self.bw_cur is None:
                if self.bw_next >= len(self.bw_jobs):
                    return
                ji = self.bw_next
                self.bw_next += 1
                srct, dstt, r0, rh, c0, cw = self.bw_jobs[ji]
                if self.bw_dve[0]:
                    g.wait_ge(self.sem("bw_vc"), self.bw_dve[0])
                if self.cur("bw_ldi0"):
                    g.wait_ge(self.sem("bw_ldi0"), self.cur("bw_ldi0"))
                sl, vl = self.bump("bw_ldi0", 16)
                g.dma_start(self.bw_idx[0][0:rh, 0:cw],
                            srct[r0:r0 + rh, c0:c0 + cw]).then_inc(sl, 16)
                self.bw_cur = (ji, dstt, r0, rh, c0, cw, sl, vl)
                self.bw_step = 0
            ji, dstt, r0, rh, c0, cw, sl, vl = self.bw_cur
            idx = self.bw_idx[0][0:rh, 0:cw]
            acc = self.bw_acc[0:rh, 0:cw]
            tmp = self.bw_tmp[0:rh, 0:cw]
            vv = self.bw_step
            if vv == 0:
                v.wait_ge(sl, vl)
                if ji == 0:
                    v.wait_ge(self.sem("bw_vc"), self.bw_lut_ready)
                if self.bw_sta:
                    v.wait_ge(self.sem("bw_sta"), self.bw_sta)
                s0, _ = self.bump("bw_vc")
                v.tensor_scalar(acc, idx, 0.0, self.lutb_sb[0:rh, 0:1],
                                ALU.is_equal, ALU.mult).then_inc(s0, 1)
                nops -= 1
            else:
                s1, v1 = self.bump("bw_vc")
                v.tensor_scalar(tmp, idx, float(vv), self.lutb_sb[0:rh, vv:vv + 1],
                                ALU.is_equal, ALU.mult).then_inc(s1, 1)
                v.wait_ge(s1, v1)
                s2, v2 = self.bump("bw_vc")
                v.tensor_tensor(acc, acc, tmp, ALU.add).then_inc(s2, 1)
                v.wait_ge(s2, v2)
                nops -= 2
            if vv == 15:
                self.bw_dve[0] = self.cur("bw_vc")
                s.wait_ge(self.sem("bw_vc"), self.cur("bw_vc"))
                if self.bw_sta:
                    s.wait_ge(self.sem("bw_sta"), self.bw_sta)
                ss, vs = self.bump("bw_sta", 16)
                s.dma_start(dstt[r0:r0 + rh, c0:c0 + cw], acc).then_inc(ss, 16)
                self.bw_sta = vs
                self.bw_cur = None
            else:
                self.bw_step += 1
        return

    def _tr_store(self):
        if self.tr_pend is None:
            return
        g = self.q["gpsimd"]
        ji, p, dst_ap, ck, sl, vl = self.tr_pend
        self.tr_pend = None
        g.wait_ge(sl, vl)
        sst, vst = self.bump(f"tr_st{p}", 16)
        g.dma_start(dst_ap, self.tr_buf[p][:, 0:ck]).then_inc(sst, 16)
        self.tr_st[p] = vst
        if ji == 191:
            self.tr_mark_uw = (self.tr_st[0], self.tr_st[1])
        if ji == len(self.tr_jobs) - 1:
            self.tr_mark_all = (self.tr_st[0], self.tr_st[1])

    def tr_open(self):
        self.tr_stack = ExitStack()
        self._tr_gen = getattr(self, "_tr_gen", 0) + 1
        self.tr_buf = [self.sbt(self.tr_stack, f"tr_b{i}_{self._tr_gen}", [128, 1024], BF16,
                                is_transpose_dest=True) for i in range(2)]

    def tr_close(self):
        self._tr_store()
        self.tr_stack.close()
        self.tr_stack = None

    def emit_tr(self, n, margin=6):
        if self.tr_stack is None:
            return
        s = self.q["sync"]
        for _ in range(n):
            if self.tr_next >= len(self.tr_jobs):
                break
            need, src_ap, dst_ap, ck = self.tr_jobs[self.tr_next]
            if need + margin >= self.bw_next:
                break
            ji = self.tr_next
            self.tr_next += 1
            p = ji % 2
            s.wait_ge(self.sem("bw_sta"), 16 * (need + 1))
            if self.tr_st[p]:
                s.wait_ge(self.sem(f"tr_st{p}"), self.tr_st[p])
            sl, vl = self.bump(f"tr_ltb{p}", 16)
            s.dma_start(self.tr_buf[p][:, 0:ck], src_ap, transpose=True).then_inc(sl, 16)
            self._tr_store()
            self.tr_pend = (ji, p, dst_ap, ck, sl, vl)
        return

    def p1(self):
        self.dma_entry_fence()
        c = self.cfg
        t, v, g, s = self.q["tensor"], self.q["vector"], self.q["gpsimd"], self.q["sync"]
        a = self.q["scalar"]
        NE, NL, TB = c.NE, c.NL, c.TB
        NJ = NE + NL
        st = ExitStack()
        wq_sb = self.sbt(st, "p1_wq", [128, NE * c.E], BF16)
        wc_sb = self.sbt(st, "p1_wc", [128, NE * c.DL], BF16)
        xt = [self.sbt(st, f"p1_xt{i}", [128, NE * 512], BF16, is_transpose_dest=True) for i in range(2)]
        ev = [self.sbt(st, f"p1_ev{i}", [128, 512], BF16) for i in range(4)]
        pp = [self.pst(st, f"p1_ps{i}", [128, 512]) for i in range(2)]

        sw, vw = self.bump("p1_ldw", 32)
        g.dma_start(wq_sb, self.wqT.rearrange("(a p) c -> p a c", p=128)).then_inc(sw, 16)
        g.dma_start(wc_sb, self.wdkvT.rearrange("(a p) c -> p a c", p=128)).then_inc(sw, 16)

        xcbuf = [self.sbt(st, f"p1_xc{i}", [128, c.E], BF16) for i in range(4)]
        xt_free = [0, 0]
        ev_st = [0, 0, 0, 0]
        for tb in range(TB):
            p = tb % 2
            for ri in range(4):
                i = tb * 4 + ri
                pb = i % 4
                if i >= 4:
                    g.wait_ge(self.sem(f"p1_stx{pb}"), 16 * (i // 4))
                if self.cur(f"p1_ld{pb}"):
                    g.wait_ge(self.sem(f"p1_ld{pb}"), self.cur(f"p1_ld{pb}"))
                slc, vlc2 = self.bump(f"p1_ld{pb}", 16)
                g.dma_start(xcbuf[pb], self.x[i * 128:(i + 1) * 128, :]).then_inc(slc, 16)
                s.wait_ge(slc, vlc2)
                if i >= 4:
                    s.wait_ge(self.sem(f"p1_stx{pb}"), 16 * (i // 4))
                ssx, _ = self.bump(f"p1_stx{pb}", 16)
                s.dma_start(self.xb[i * 128:(i + 1) * 128, :], xcbuf[pb]).then_inc(ssx, 16)
            if xt_free[p]:
                a.wait_ge(self.sem("p1_mm"), xt_free[p])
            for pb in range(4):
                a.wait_ge(self.sem(f"p1_stx{pb}"), self.cur(f"p1_stx{pb}"))
            if self.cur(f"p1_ldx{p}"):
                a.wait_ge(self.sem(f"p1_ldx{p}"), self.cur(f"p1_ldx{p}"))
            slx, vlx = self.bump(f"p1_ldx{p}", 16 * NE)
            for e_i in range(NE):
                a.dma_start(xt[p][:, e_i * 512:(e_i + 1) * 512],
                            self.xb[tb * 512:(tb + 1) * 512, e_i * 128:(e_i + 1) * 128],
                            transpose=True).then_inc(slx, 16)
            t.wait_ge(slx, vlx)
            if tb == 0:
                t.wait_ge(sw, vw)
                t.wait_ge(self.sem("bw_vc"), self.bw_lut_ready)
            for j in range(NJ):
                pj = pp[j % 2]
                need = self.cur(f"p1_evv{j % 2}")
                if need:
                    t.wait_ge(self.sem(f"p1_evv{j % 2}"), need)
                for e_i in range(NE):
                    if j < NE:
                        lhs = wq_sb[:, e_i * c.E + j * 128: e_i * c.E + (j + 1) * 128]
                    else:
                        jj = j - NE
                        lhs = wc_sb[:, e_i * c.DL + jj * 128: e_i * c.DL + (jj + 1) * 128]
                    mi = t.matmul(pj, lhs, xt[p][:, e_i * 512:(e_i + 1) * 512],
                                  start=(e_i == 0), stop=(e_i == NE - 1))
                smm, vmm = self.bump("p1_mm")
                mi.then_inc(smm, 1)
                v.wait_ge(smm, vmm)
                k = j % 4
                if ev_st[k]:
                    v.wait_ge(self.sem(f"p1_st{k}"), ev_st[k])
                sev, vev = self.bump(f"p1_evv{j % 2}")
                v.tensor_copy(ev[k], pj).then_inc(sev, 1)
                s.wait_ge(sev, vev)
                if ev_st[k]:
                    s.wait_ge(self.sem(f"p1_st{k}"), ev_st[k])
                sst, vst = self.bump(f"p1_st{k}", 16)
                dst = (self.QT0[j * 128:(j + 1) * 128, tb * 512:(tb + 1) * 512] if j < NE
                       else self.CT0[(j - NE) * 128:(j - NE + 1) * 128, tb * 512:(tb + 1) * 512])
                s.dma_start(dst, ev[k]).then_inc(sst, 16)
                ev_st[k] = vst
                self.emit_build_ops(3)
            xt_free[p] = self.cur("p1_mm")
        self.p1_done = self.phase_done("p1")
        self.last_done = self.p1_done
        st.close()

    def p2(self):
        self.dma_entry_fence()
        self.tr_open()
        c = self.cfg
        t, v, sc, g, s = (self.q["tensor"], self.q["vector"], self.q["scalar"],
                          self.q["gpsimd"], self.q["sync"])
        NE, NL, NT, TB = c.NE, c.NL, c.NT, c.TB
        st = ExitStack()
        ct_sb = self.sbt(st, "p2_ct", [128, NL * c.T], BF16)
        kt_sb = self.sbt(st, "p2_kt", [128, c.T], BF16)
        vv_sb = self.sbt(st, "p2_vv", [128, NT * DH], BF16)
        qt_sb = self.sbt(st, "p2_qt", [128, c.T], BF16)
        sq_sb = self.sbt(st, "p2_sq", [128, c.T], BF16)
        kup_sb = self.sbt(st, "p2_kup", [128, NL * DH], BF16)
        vup_sb = self.sbt(st, "p2_vup", [128, NL * DH], BF16)
        p_sb = [self.sbt(st, f"p2_p{i}", [128, NT * 512], BF16) for i in range(2)]
        o_sb = [self.sbt(st, f"p2_o{i}", [128, 512], BF16) for i in range(2)]
        r_sb = [self.sbt(st, f"p2_r{i}", [1, 512], F32) for i in range(2)]
        lnt_sb = [self.sbt(st, f"p2_lnt{i}", [1, 512], F32) for i in range(2)]
        br_sb = self.sbt(st, "p2_br", [128, 512], F32)
        mx_sb = self.sbt(st, "p2_mx", [1, 16], F32)
        biasb = self.sbt(st, "p2_bias", [128, 1], F32)

        sp_pair = [self.pst(st, f"p2_s{i}", [128, 1024]) for i in range(2)]
        sp = [sp_pair[0][:, 0:512], sp_pair[1][:, 0:512]]
        op_ps = [self.pst(st, f"p2_op{i}", [128, 512]) for i in range(2)]
        aux_ps = self.pst(st, "p2_aux", [128, 512])
        sum_ps = [aux_ps[0:1, :], aux_ps[32:33, :]]
        nrm_ps = aux_ps[64:65, :]
        br_ps = self.pst(st, "p2_brp", [128, 512])
        nrm2_ps = br_ps[0:1, :]
        kv_ps = [op_ps[0][:, 0:DH], op_ps[1][:, 0:DH]]

        if self.cur("p2_ldc"):
            s.wait_ge(self.sem("p2_ldc"), self.cur("p2_ldc"))
        slc, vlc = self.bump("p2_ldc", 16 * NL)
        self.fence(s, self.p1_done)
        self.fence(g, self.p1_done)
        for l_i in range(NL):
            s.dma_start(ct_sb[:, l_i * c.T:(l_i + 1) * c.T],
                        self.CT0[l_i * 128:(l_i + 1) * 128, :]).then_inc(slc, 16)
        t.wait_ge(slc, vlc)
        t.wait_ge(self.sem(self.const_ready[0]), self.const_ready[1])

        nchunk = c.T // 512
        o_st = [0, 0]
        rv_used = [0, 0]
        nrm_rd = [0, 0]
        pvmm = [0, 0]
        for h in range(c.NH):
            mm_prev = self.cur("p2_mm")
            ac_prev = self.cur("p2_ac")
            g.wait_ge(self.sem("p2_mm"), mm_prev)
            swh, vwh = self.bump("p2_ldw", 32)
            g.dma_start(kup_sb, self.wupkT[h].rearrange("(a p) c -> p a c", p=128)).then_inc(swh, 16)
            g.dma_start(vup_sb, self.wupvT[h].rearrange("(a p) c -> p a c", p=128)).then_inc(swh, 16)
            if self.cur("p2_ldq"):
                g.wait_ge(self.sem("p2_ldq"), self.cur("p2_ldq"))
            sqh, vqh = self.bump("p2_ldq", 16)
            g.dma_start(qt_sb, self.QT0[h * 128:(h + 1) * 128, :]).then_inc(sqh, 16)

            t.wait_ge(swh, vwh)
            t.wait_ge(self.sem("p2_ac"), ac_prev)
            for tb in range(TB):
                pj = sp[tb % 2]
                if tb >= 2:
                    t.wait_ge(self.sem("p2_kte"), self.cur("p2_kte") - 1)
                for l_i in range(NL):
                    mi = t.matmul(pj, kup_sb[:, l_i * DH:(l_i + 1) * DH],
                                  ct_sb[:, l_i * c.T + tb * 512: l_i * c.T + (tb + 1) * 512],
                                  start=(l_i == 0), stop=(l_i == NL - 1))
                smm, vmm = self.bump("p2_mm")
                mi.then_inc(smm, 1)
                sc.wait_ge(smm, vmm)
                if h > 0 and tb == 0:
                    sc.wait_ge(self.sem("p2_mm"), mm_prev)
                ske, _ = self.bump("p2_kte")
                sc.copy(kt_sb[:, tb * 512:(tb + 1) * 512], pj).then_inc(ske, 1)
            if self.cur("p2_pv0"):
                t.wait_ge(self.sem("p2_pv0"), self.cur("p2_pv0"))
            if self.cur("p2_pv1"):
                t.wait_ge(self.sem("p2_pv1"), self.cur("p2_pv1"))
            for tt in range(NT):
                pk = kv_ps[tt % 2]
                if tt >= 2:
                    t.wait_ge(self.sem("p2_ve"), self.cur("p2_ve") - 1)
                for l_i in range(NL):
                    mi = t.matmul(pk,
                                  ct_sb[:, l_i * c.T + tt * 128: l_i * c.T + (tt + 1) * 128],
                                  vup_sb[:, l_i * DH:(l_i + 1) * DH],
                                  start=(l_i == 0), stop=(l_i == NL - 1))
                smm, vmm = self.bump("p2_mm")
                mi.then_inc(smm, 1)
                sc.wait_ge(smm, vmm)
                if h > 0 and tt == 0:
                    sc.wait_ge(self.sem("p2_mm"), mm_prev)
                sve, _ = self.bump("p2_ve")
                sc.copy(vv_sb[:, tt * DH:(tt + 1) * DH], pk).then_inc(sve, 1)

            sc.wait_ge(sqh, vqh)
            if h > 0:
                sc.wait_ge(self.sem("p2_mm"), mm_prev)
            sa, va = self.bump("p2_ac")
            sc.activation(sq_sb, qt_sb, AF.Square).then_inc(sa, 1)
            t.wait_ge(self.sem("p2_vc"), self.cur("p2_vc"))
            for ci in range(nchunk):
                t.wait_ge(sa, va)
                bk = ci % 2
                nb = nrm_ps if bk == 0 else nrm2_ps
                if nrm_rd[bk]:
                    t.wait_ge(self.sem("p2_nrx"), nrm_rd[bk])
                mi = t.matmul(nb, self.ones_sb, sq_sb[:, ci * 512:(ci + 1) * 512],
                              start=True, stop=True)
                smm, vmm = self.bump("p2_mm")
                mi.then_inc(smm, 1)
                v.wait_ge(smm, vmm)
                sx, vx = self.bump("p2_nrx")
                v.reduce_max(mx_sb[:, ci:ci + 1], nb, axis=AX).then_inc(sx, 1)
                nrm_rd[bk] = vx
            ksq_gate = self.cur("p2_mm")
            sc.wait_ge(self.sem("p2_mm"), ksq_gate)
            sc.wait_ge(self.sem("p2_kte"), self.cur("p2_kte"))
            ska, vka = self.bump("p2_ac")
            sc.activation(sq_sb, kt_sb, AF.Square).then_inc(ska, 1)
            for ci in range(nchunk):
                t.wait_ge(ska, vka)
                bk = ci % 2
                nb = nrm_ps if bk == 0 else nrm2_ps
                if nrm_rd[bk]:
                    t.wait_ge(self.sem("p2_nrx"), nrm_rd[bk])
                mi = t.matmul(nb, self.ones_sb, sq_sb[:, ci * 512:(ci + 1) * 512],
                              start=True, stop=True)
                smm, vmm = self.bump("p2_mm")
                mi.then_inc(smm, 1)
                v.wait_ge(smm, vmm)
                sx, vx = self.bump("p2_nrx")
                v.reduce_max(mx_sb[:, 8 + ci:8 + ci + 1], nb, axis=AX).then_inc(sx, 1)
                nrm_rd[bk] = vx
            v.wait_ge(self.sem("p2_nrx"), self.cur("p2_nrx"))
            sv1, _ = self.bump("p2_vc")
            v.reduce_max(mx_sb[:, 14:15], mx_sb[:, 0:nchunk], axis=AX).then_inc(sv1, 1)
            sv2, vv2 = self.bump("p2_vc")
            v.reduce_max(mx_sb[:, 15:16], mx_sb[:, 8:8 + nchunk], axis=AX).then_inc(sv2, 1)
            v.wait_ge(sv2, vv2)
            sv3, vv3 = self.bump("p2_vc")
            v.tensor_tensor(mx_sb[:, 13:14], mx_sb[:, 14:15], mx_sb[:, 15:16], ALU.mult).then_inc(sv3, 1)
            sc.wait_ge(sv3, vv3)
            sa4, va4 = self.bump("p2_ac")
            sc.activation(mx_sb[:, 12:13], mx_sb[:, 13:14], AF.Sqrt).then_inc(sa4, 1)
            sc.wait_ge(sa4, va4)
            sa5, va5 = self.bump("p2_ac")
            sc.mul(mx_sb[:, 11:12], mx_sb[:, 12:13], -1.01 * ISCALE).then_inc(sa5, 1)
            t.wait_ge(sa5, va5)
            mi = t.matmul(br_ps[:, 0:1], self.onesr_sb, mx_sb[:, 11:12], start=True, stop=True)
            smm, vmm = self.bump("p2_mm")
            mi.then_inc(smm, 1)
            sc.wait_ge(smm, vmm)
            sbb, vbb = self.bump("p2_bv")
            sc.copy(biasb, br_ps[:, 0:1]).then_inc(sbb, 1)
            sc.wait_ge(sbb, vbb)

            t.wait_ge(self.sem("p2_kte"), self.cur("p2_kte"))
            pend = None
            prev = None
            br_ev = [0]

            def fin2(qb_, rv_):
                pqb_ = qb_ % 2
                t.wait_ge(self.sem("p2_ac"), rv_)
                if br_ev[0]:
                    t.wait_ge(self.sem("p2_vc"), br_ev[0])
                mi_ = t.matmul(br_ps, self.onesr_sb, r_sb[pqb_], start=True, stop=True)
                smm_, vmm_ = self.bump("p2_mm")
                mi_.then_inc(smm_, 1)
                v.wait_ge(smm_, vmm_)
                for pp_ in range(2):
                    if self.cur(f"p2_pv{pp_}"):
                        v.wait_ge(self.sem(f"p2_pv{pp_}"), self.cur(f"p2_pv{pp_}"))
                se_, ve_ = self.bump("p2_vc")
                v.tensor_copy(br_sb, br_ps).then_inc(se_, 1)
                br_ev[0] = ve_
                v.wait_ge(se_, ve_)
                if o_st[pqb_]:
                    v.wait_ge(self.sem(f"p2_st{pqb_}"), o_st[pqb_])
                so_, vo_ = self.bump(f"p2_pv{pqb_}")
                v.tensor_tensor(o_sb[pqb_], op_ps[pqb_], br_sb, ALU.mult).then_inc(so_, 1)
                s.wait_ge(so_, vo_)
                if o_st[pqb_]:
                    s.wait_ge(self.sem(f"p2_st{pqb_}"), o_st[pqb_])
                sst_, vst_ = self.bump(f"p2_st{pqb_}", 16)
                s.dma_start(self.OT0[h * 128:(h + 1) * 128, qb_ * 512:(qb_ + 1) * 512],
                            o_sb[pqb_]).then_inc(sst_, 16)
                o_st[pqb_] = vst_

            def pv_tile(qb_, eb_, tt_, first, last):
                pq_ = qb_ % 2
                if first:
                    if self.cur(f"p2_pv{pq_}"):
                        t.wait_ge(self.sem(f"p2_pv{pq_}"), self.cur(f"p2_pv{pq_}"))
                    if rv_used[pq_]:
                        t.wait_ge(self.sem("p2_ac"), rv_used[pq_])
                t.matmul(op_ps[pq_], vv_sb[:, tt_ * DH:(tt_ + 1) * DH],
                         p_sb[pq_][:, tt_ * 512:(tt_ + 1) * 512],
                         start=first, stop=last, skip_group_check=True)
                mi_ = t.matmul(sum_ps[pq_], self.ones_sb, p_sb[pq_][:, tt_ * 512:(tt_ + 1) * 512],
                               start=first, stop=last, skip_group_check=True)
                if last:
                    smm_, vmm_ = self.bump("p2_mm")
                    mi_.then_inc(smm_, 1)
                    return vmm_
                return None

            def b_finalize(qb_, vmm_):
                nonlocal pend
                pq_ = qb_ % 2
                sc.wait_ge(self.sem("p2_mm"), vmm_)
                sr, vr = self.bump("p2_ac")
                sc.activation(lnt_sb[pq_], sum_ps[pq_], AF.Ln).then_inc(sr, 1)
                sc.wait_ge(sr, vr)
                sr2, vr2 = self.bump("p2_ac")
                sc.activation(r_sb[pq_], lnt_sb[pq_], AF.Exp, scale=-1.0).then_inc(sr2, 1)
                rv_used[pq_] = vr2
                pvmm[pq_] = vmm_
                if pend is not None:
                    fin2(*pend)
                pend = (qb_, vr2)

            NP = NT // 2
            for qb in range(TB):
                pqb = qb % 2
                if pvmm[pqb]:
                    sc.wait_ge(self.sem("p2_mm"), pvmm[pqb])
                exp_base = self.cur("p2_ac")
                bvmm = None
                for pr in range(NP):
                    tt0 = 2 * pr
                    pj = sp_pair[pr % 2]
                    if pr >= 2:
                        t.wait_ge(self.sem("p2_ac"), exp_base + pr - 1)
                    elif pr == 0 and prev is not None:
                        t.wait_ge(self.sem("p2_ac"), prev[1] + NP)
                    mi = t.matmul(pj[:, 0:512], kt_sb[:, tt0 * 128:(tt0 + 1) * 128],
                                  qt_sb[:, qb * 512:(qb + 1) * 512], start=True, stop=True)
                    mi = t.matmul(pj[:, 512:1024], kt_sb[:, (tt0 + 1) * 128:(tt0 + 2) * 128],
                                  qt_sb[:, qb * 512:(qb + 1) * 512], start=True, stop=True)
                    smm, vmm = self.bump("p2_mm")
                    mi.then_inc(smm, 1)
                    sc.wait_ge(smm, vmm)
                    sae, _ = self.bump("p2_ac")
                    sc.activation(p_sb[pqb][:, tt0 * 512:(tt0 + 2) * 512], pj, AF.Exp,
                                  bias=biasb, scale=ISCALE).then_inc(sae, 1)
                    if prev is not None:
                        for d_ in (0, 1):
                            ttp = tt0 + d_
                            r_ = pv_tile(prev[0], prev[1], ttp, ttp == 0, ttp == NT - 1)
                            if r_ is not None:
                                bvmm = r_
                    self.emit_build_ops(2)
                if prev is not None:
                    b_finalize(prev[0], bvmm)
                prev = (qb, exp_base)
                self.emit_tr(1)
            qb_, eb_ = prev
            bvmm = None
            for tt in range(NT):
                if tt % 2 == 0:
                    t.wait_ge(self.sem("p2_ac"), eb_ + tt // 2 + 1)
                r_ = pv_tile(qb_, eb_, tt, tt == 0, tt == NT - 1)
                if r_ is not None:
                    bvmm = r_
            b_finalize(qb_, bvmm)
            if pend is not None:
                fin2(*pend)
                pend = None
            prev = None
            self.emit_tr(1)
        self.p2_done = self.phase_done("p2")
        self.last_done = self.p2_done
        st.close()
        self.tr_close()

    def p3(self):
        self.dma_entry_fence()
        c = self.cfg
        t, v, g, s = self.q["tensor"], self.q["vector"], self.q["gpsimd"], self.q["sync"]
        a = self.q["scalar"]
        NE, TB = c.NE, c.TB
        st = ExitStack()
        wo_sb = self.sbt(st, "p3_wo", [128, NE * c.E], BF16)
        ot = [self.sbt(st, f"p3_ot{i}", [128, NE * 512], BF16) for i in range(2)]
        xt = [self.sbt(st, "p3_xt0", [128, NE * 512], BF16, is_transpose_dest=True)]
        xt = [xt[0], xt[0]]
        ap = [self.pst(st, f"p3_ap{i}", [128, 512]) for i in range(3)]
        sum_ps = self.pst(st, "p3_sum", [1, 512])
        sq_ps = self.pst(st, "p3_sq", [1, 512])
        mb_ps = self.pst(st, "p3_mbp", [128, 512])
        rb_ps = self.pst(st, "p3_rbp", [128, 512])
        lnr = self.make_ln_res(st, "p3", sum_ps, sq_ps, mb_ps, rb_ps, BF16, depth=2)

        if self.cur("p3_ldw"):
            g.wait_ge(self.sem("p3_ldw"), self.cur("p3_ldw"))
        sw, vw = self.bump("p3_ldw", 16)
        self.fence(g, self.p2_done)
        g.dma_start(wo_sb, self.woT.rearrange("(a p) c -> p a c", p=128)).then_inc(sw, 16)

        ot_free = [0, 0]
        xt_free = [0, 0]
        self._p3_xt_last = None
        self.fence(s, self.p2_done)
        self.fence(a, self.p2_done)
        ld_vals = {}

        def emit_loads(tb):
            p = tb % 2
            if ot_free[p]:
                g.wait_ge(self.sem("p3_mm"), ot_free[p])
            if self.cur(f"p3_ldo{p}"):
                g.wait_ge(self.sem(f"p3_ldo{p}"), self.cur(f"p3_ldo{p}"))
            slo, vlo = self.bump(f"p3_ldo{p}", 16 * NE)
            for e_i in range(NE):
                g.dma_start(ot[p][:, e_i * 512:(e_i + 1) * 512],
                            self.OT0[e_i * 128:(e_i + 1) * 128,
                                     tb * 512:(tb + 1) * 512]).then_inc(slo, 16)
            for bi in range(3):
                if self.cur(f"p3_zadd{bi}"):
                    s.wait_ge(self.sem(f"p3_zadd{bi}"), self.cur(f"p3_zadd{bi}"))
            if self.cur("p3_ltb0"):
                s.wait_ge(self.sem("p3_ltb0"), self.cur("p3_ltb0"))
            slx, vlx = self.bump("p3_ltb0", 16 * NE)
            for e_i in range(NE):
                s.dma_start(xt[p][:, e_i * 512:(e_i + 1) * 512],
                            self.xb[tb * 512:(tb + 1) * 512, e_i * 128:(e_i + 1) * 128],
                            transpose=True).then_inc(slx, 16)
            ld_vals[tb] = (slo, vlo, slx, vlx)

        emit_loads(0)
        for tb in range(TB):
            p = tb % 2
            slo, vlo, slx, vlx = ld_vals[tb]
            t.wait_ge(slo, vlo)
            if tb == 0:
                t.wait_ge(sw, vw)
            v.wait_ge(slx, vlx)
            lnr["cur_par"] = tb % 2
            for j in range(NE):
                bk = j % 3
                need = self.cur(f"p3_zadd{bk}")
                if need:
                    t.wait_ge(self.sem(f"p3_zadd{bk}"), need)
                for e_i in range(NE):
                    mi = t.matmul(ap[bk],
                                  wo_sb[:, e_i * c.E + j * 128: e_i * c.E + (j + 1) * 128],
                                  ot[p][:, e_i * 512:(e_i + 1) * 512],
                                  start=(e_i == 0), stop=(e_i == NE - 1), skip_group_check=True)
                smm, vmm = self.bump("p3_mm")
                mi.then_inc(smm, 1)
                self.ln_z("p3", lnr, j, ap[bk], [(smm, vmm)],
                          xt[p][:, j * 512:(j + 1) * 512], bank=bk)
                self.emit_build_ops(4)
            if tb + 1 < TB:
                emit_loads(tb + 1)
            if tb > 0:
                lnr["norm_par"] = (tb - 1) % 2
                self.ln_norm("p3", lnr, tb - 1, self.X1)
            self.ln_stats("p3", lnr, tb)
            ot_free[p] = self.cur("p3_mm")
            xt_free[p] = (self.cur("p3_zadd0"), self.cur("p3_zadd1"), self.cur("p3_zadd2"))
            self._p3_xt_last = xt_free[p]
        lnr["norm_par"] = (TB - 1) % 2
        self.ln_norm("p3", lnr, TB - 1, self.X1)
        self.p3_done = self.phase_done("p3")
        self.last_done = self.p3_done
        st.close()

    def p4(self):
        self.dma_entry_fence()
        c = self.cfg
        t, v, sc, g, s = (self.q["tensor"], self.q["vector"], self.q["scalar"],
                          self.q["gpsimd"], self.q["sync"])
        NE, TB = c.NE, c.TB
        st = ExitStack()
        wf_sb = self.sbt(st, "p4_wf", [128, NE * c.E], BF16)
        aw_sb = self.sbt(st, "p4_aw", [128, NE * 128], BF16)
        bw_sb = self.sbt(st, "p4_bw", [128, c.E], BF16)
        gw_sb = self.sbt(st, "p4_gw", [128, NE * NEXP], BF16)
        x1t = [self.sbt(st, f"p4_x1{i}", [128, NE * 512], BF16) for i in range(2)]
        lg_sb = self.sbt(st, "p4_lg", [128, 4 * NEXP], F32)
        gsc = self.sbt(st, "p4_gsc", [128, 16], F32)
        exg_sb = self.sbt(st, "p4_exg", [128, NEXP], F32)
        cwn_sb = self.sbt(st, "p4_cwn", [128, NEXP], BF16)
        cwt_sb = self.sbt(st, "p4_cwt", [NEXP, 512], BF16)
        cwx_sb = self.sbt(st, "p4_cwx", [128, 512], BF16)
        t1s_sb = self.sbt(st, "p4_t1s", [128, 512], BF16)

        out_ps = [self.pst(st, f"p4_out{i}", [128, 512]) for i in range(3)]
        t1_ps = self.pst(st, "p4_t1", [128, 512])
        u_ps = self.pst(st, "p4_u", [128, 512])
        u_bf = self.pst(st, "p4_ubf", [128, 512], BF16)
        sum_ps = self.pst(st, "p4_sum", [1, 512])
        sq_ps = self.pst(st, "p4_sq", [1, 512])
        lnr = self.make_ln_res(st, "p4", sum_ps, sq_ps, t1_ps, u_ps, BF16, depth=2)

        if self.cur("p4_ldw"):
            g.wait_ge(self.sem("p4_ldw"), self.cur("p4_ldw"))
        sw, vw = self.bump("p4_ldw", 16 * 4)
        self.fence(g, self.p3_done)
        g.dma_start(wf_sb, self.wffT.rearrange("(a p) c -> p a c", p=128)).then_inc(sw, 16)
        g.dma_start(aw_sb, self.awT.rearrange("(a p) c -> p a c", p=128)).then_inc(sw, 16)
        g.dma_start(bw_sb, self.bwT).then_inc(sw, 16)
        g.dma_start(gw_sb, self.gatewT.rearrange("(a p) c -> p a c", p=128)).then_inc(sw, 16)

        x1_free = [0, 0]
        self.fence(s, self.p3_done)
        for tb in range(TB):
            p = tb % 2
            if x1_free[p]:
                g.wait_ge(self.sem("p4_mm"), x1_free[p][0])
                for bi in range(3):
                    if x1_free[p][1 + bi]:
                        g.wait_ge(self.sem(f"p4_zadd{bi}"), x1_free[p][1 + bi])
            if self.cur(f"p4_ldo{p}"):
                g.wait_ge(self.sem(f"p4_ldo{p}"), self.cur(f"p4_ldo{p}"))
            slx, vlx = self.bump(f"p4_ldo{p}", 16 * NE)
            for e_i in range(NE):
                g.dma_start(x1t[p][:, e_i * 512:(e_i + 1) * 512],
                            self.X1[e_i * 128:(e_i + 1) * 128,
                                    tb * 512:(tb + 1) * 512]).then_inc(slx, 16)
            t.wait_ge(slx, vlx)
            if tb == 0:
                t.wait_ge(sw, vw)
                t.wait_ge(self.sem(self.const_ready[0]), self.const_ready[1])

            if getattr(self, "_p4_ups_free", 0):
                t.wait_ge(self.sem("p4_vc"), self._p4_ups_free)
            if lnr.get("free_wait"):
                t.wait_ge(*lnr["free_wait"])
            for ts in range(4):
                for e_i in range(NE):
                    mi = t.matmul(u_ps[:, ts * NEXP:(ts + 1) * NEXP],
                                  x1t[p][:, e_i * 512 + ts * 128: e_i * 512 + (ts + 1) * 128],
                                  gw_sb[:, e_i * NEXP:(e_i + 1) * NEXP],
                                  start=(e_i == 0), stop=(e_i == NE - 1))
            smm, vmm = self.bump("p4_mm")
            mi.then_inc(smm, 1)
            v.wait_ge(smm, vmm)
            if getattr(self, "_p4_lg_readers", 0):
                v.wait_ge(self.sem("p4_ac"), self._p4_lg_readers)
            sv, vv = self.bump("p4_vc")
            v.tensor_copy(lg_sb, u_ps[:, 0:4 * NEXP]).then_inc(sv, 1)
            for ts in range(4):
                lg = lg_sb[:, ts * NEXP:(ts + 1) * NEXP]
                v.wait_ge(sv, vv)
                s1, v1 = self.bump("p4_vc")
                v.reduce_max(gsc[:, 0:1], lg, axis=AX).then_inc(s1, 1)
                v.wait_ge(s1, v1)
                s2, v2 = self.bump("p4_vc")
                v.tensor_scalar(gsc[:, 2:3], gsc[:, 0:1], -1.0, None, ALU.mult).then_inc(s2, 1)
                s3, v3 = self.bump("p4_vc")
                v.tensor_scalar(exg_sb, lg, gsc[:, 0:1], -1e30, ALU.is_equal, ALU.mult).then_inc(s3, 1)
                v.wait_ge(s3, v3)
                s4, v4 = self.bump("p4_vc")
                v.tensor_tensor(exg_sb, lg, exg_sb, ALU.add).then_inc(s4, 1)
                v.wait_ge(s4, v4)
                s5, v5 = self.bump("p4_vc")
                v.reduce_max(gsc[:, 1:2], exg_sb, axis=AX).then_inc(s5, 1)
                sc.wait_ge(s5, v5)
                sa1, va1 = self.bump("p4_ac")
                sc.activation(exg_sb, lg, AF.Exp, bias=gsc[:, 2:3]).then_inc(sa1, 1)
                v.wait_ge(sa1, va1)
                if getattr(self, "_p4_tr_last", 0):
                    v.wait_ge(self.sem("p4_mm"), self._p4_tr_last)
                s6, v6 = self.bump("p4_vc")
                v.tensor_scalar(cwn_sb, lg, gsc[:, 1:2], None, ALU.is_ge).then_inc(s6, 1)
                v.wait_ge(s6, v6)
                s7, v7 = self.bump("p4_vc")
                v.tensor_tensor(exg_sb, exg_sb, cwn_sb, ALU.mult).then_inc(s7, 1)
                v.wait_ge(s7, v7)
                s8, v8 = self.bump("p4_vc")
                v.reduce_sum(gsc[:, 3:4], exg_sb, axis=AX).then_inc(s8, 1)
                v.wait_ge(s8, v8)
                s9, v9 = self.bump("p4_vc")
                v.reciprocal(gsc[:, 4:5], gsc[:, 3:4]).then_inc(s9, 1)
                v.wait_ge(s9, v9)
                s10, v10 = self.bump("p4_vc")
                v.tensor_scalar(cwn_sb, exg_sb, gsc[:, 4:5], None, ALU.mult).then_inc(s10, 1)
                t.wait_ge(s10, v10)
                if self.cur("p4_vg") > 0:
                    t.wait_ge(self.sem("p4_vg"), self.cur("p4_vg"))
                mi = t.transpose(u_bf[0:NEXP, ts * 128:(ts + 1) * 128], cwn_sb, self.id_sb)
                smm, vmm = self.bump("p4_mm")
                mi.then_inc(smm, 1)
                self._p4_tr_last = vmm
                v.wait_ge(smm, vmm)
                s11, v11 = self.bump("p4_vg")
                v.tensor_copy(cwt_sb[:, ts * 128:(ts + 1) * 128],
                              u_bf[0:NEXP, ts * 128:(ts + 1) * 128]).then_inc(s11, 1)
            v16 = self.cur("p4_vg")
            t.wait_ge(self.sem("p4_vg"), v16)
            mi = t.matmul(u_ps, self.e16_sb, cwt_sb, start=True, stop=True)
            smm, vmm = self.bump("p4_mm")
            mi.then_inc(smm, 1)
            need = self.cur("p4_t1e")
            if need:
                t.wait_ge(self.sem("p4_t1e"), need)
            for e_i in range(NE):
                mi2 = t.matmul(t1_ps, aw_sb[:, e_i * 128:(e_i + 1) * 128],
                               x1t[p][:, e_i * 512:(e_i + 1) * 512],
                               start=(e_i == 0), stop=(e_i == NE - 1))
            smm2, vmm2 = self.bump("p4_mm")
            mi2.then_inc(smm2, 1)
            v.wait_ge(smm, vmm)
            sx, vx = self.bump("p4_vc")
            v.tensor_copy(cwx_sb, u_ps).then_inc(sx, 1)
            self._p4_ups_free = vx
            self._p4_lg_readers = self.cur("p4_ac")
            v.wait_ge(smm2, vmm2)
            v.wait_ge(sx, vx)
            st1, vt1 = self.bump("p4_t1e")
            v.tensor_tensor(t1s_sb, t1_ps, cwx_sb, ALU.mult).then_inc(st1, 1)

            lnr["cur_par"] = tb % 2
            for j in range(NE):
                bk = j % 3
                need = self.cur(f"p4_zadd{bk}")
                if need:
                    t.wait_ge(self.sem(f"p4_zadd{bk}"), need)
                for e_i in range(NE):
                    t.matmul(out_ps[bk],
                             wf_sb[:, e_i * c.E + j * 128: e_i * c.E + (j + 1) * 128],
                             x1t[p][:, e_i * 512:(e_i + 1) * 512],
                             start=(e_i == 0), stop=False, skip_group_check=True)
                if j == 0:
                    t.wait_ge(st1, vt1)
                mi = t.matmul(out_ps[bk], bw_sb[:, j * 128:(j + 1) * 128], t1s_sb,
                              start=False, stop=True, skip_group_check=True)
                smm, vmm = self.bump("p4_mm")
                mi.then_inc(smm, 1)
                self.ln_z("p4", lnr, j, out_ps[bk], [(smm, vmm)],
                          x1t[p][:, j * 512:(j + 1) * 512], bank=bk)
                self.emit_build_ops(4)
            if tb > 0:
                lnr["norm_par"] = (tb - 1) % 2
                lnr["extra_norm_wait"] = [(st1, vt1),
                                          (self.sem("p4_vc"), self._p4_ups_free)]
                self.ln_norm("p4", lnr, tb - 1, self.X2)
            self.ln_stats("p4", lnr, tb)
            x1_free[p] = (self.cur("p4_mm"), self.cur("p4_zadd0"),
                          self.cur("p4_zadd1"), self.cur("p4_zadd2"))
        lnr["norm_par"] = (TB - 1) % 2
        lnr["extra_norm_wait"] = []
        self.ln_norm("p4", lnr, TB - 1, self.X2)
        self.p4_done = self.phase_done("p4")
        self.last_done = self.p4_done
        st.close()

    def p5(self):
        self.dma_entry_fence()
        self.tr_open()
        c = self.cfg
        t, v, g, s = self.q["tensor"], self.q["vector"], self.q["gpsimd"], self.q["sync"]
        NE, TB, NG = c.NE, c.TB, c.NG
        st = ExitStack()
        wq_sb = self.sbt(st, "p5_wq", [128, c.NH * NE * DH], BF16)
        wk_sb = self.sbt(st, "p5_wk", [128, NG * NE * DH], BF16)
        wv_sb = self.sbt(st, "p5_wv", [128, NG * NE * DH], BF16)
        bq_sb = self.sbt(st, "p5_bq", [128, c.NH], F32)
        bk_sb = self.sbt(st, "p5_bk", [128, NG], F32)
        bv_sb = self.sbt(st, "p5_bv", [128, NG], F32)
        x2t = [self.sbt(st, f"p5_x2{i}", [128, NE * 512], BF16) for i in range(2)]
        ev = [self.sbt(st, f"p5_ev{i}", [128, 512], BF16) for i in range(4)]
        vt_sb = self.sbt(st, "p5_vt", [128, 512], BF16)
        tv_sb = [self.sbt(st, f"p5_tv{i}", [128, 128], BF16) for i in range(2)]

        pp = [self.pst(st, f"p5_ps{i}", [128, 512]) for i in range(2)]
        tr_ps = [self.pst(st, f"p5_tr{i}", [128, 128], BF16) for i in range(2)]

        if self.cur("p5_ldw"):
            g.wait_ge(self.sem("p5_ldw"), self.cur("p5_ldw"))
        sw, vw = self.bump("p5_ldw", 16 * (c.NH + 2 * NG + 3))
        self.fence(g, self.p4_done)
        for h in range(c.NH):
            g.dma_start(wq_sb[:, h * NE * DH:(h + 1) * NE * DH],
                        self.wq1[h].rearrange("(a p) c -> p a c", p=128)).then_inc(sw, 16)
        for gg in range(NG):
            g.dma_start(wk_sb[:, gg * NE * DH:(gg + 1) * NE * DH],
                        self.wk1[gg].rearrange("(a p) c -> p a c", p=128)).then_inc(sw, 16)
            g.dma_start(wv_sb[:, gg * NE * DH:(gg + 1) * NE * DH],
                        self.wv1[gg].rearrange("(a p) c -> p a c", p=128)).then_inc(sw, 16)
        g.dma_start(bq_sb, self.bq1.rearrange("a p -> p a"),
                    allow_slow_non_contiguous=True).then_inc(sw, 16)
        g.dma_start(bk_sb, self.bk1.rearrange("a p -> p a"),
                    allow_slow_non_contiguous=True).then_inc(sw, 16)
        g.dma_start(bv_sb, self.bv1.rearrange("a p -> p a"),
                    allow_slow_non_contiguous=True).then_inc(sw, 16)

        x2_free = [0, 0]
        ev_st = [0, 0, 0, 0]
        tv_st = [0, 0]
        self.fence(s, self.p4_done)
        for tb in range(TB):
            p = tb % 2
            if x2_free[p]:
                g.wait_ge(self.sem("p5_mm"), x2_free[p])
            if self.cur(f"p5_ldo{p}"):
                g.wait_ge(self.sem(f"p5_ldo{p}"), self.cur(f"p5_ldo{p}"))
            slx, vlx = self.bump(f"p5_ldo{p}", 16 * NE)
            for e_i in range(NE):
                g.dma_start(x2t[p][:, e_i * 512:(e_i + 1) * 512],
                            self.X2[e_i * 128:(e_i + 1) * 128,
                                    tb * 512:(tb + 1) * 512]).then_inc(slx, 16)
            t.wait_ge(slx, vlx)
            if tb == 0:
                t.wait_ge(sw, vw)
            for kind in range(3):
                nout = c.NH if kind == 0 else NG
                wsrc = (wq_sb, wk_sb, wv_sb)[kind]
                bias = (bq_sb, bk_sb, bv_sb)[kind]
                dest = (self.QT1, self.KT1, None)[kind]
                for jo in range(nout):
                    pj = pp[jo % 2]
                    need = self.cur(f"p5_evv{jo % 2}")
                    if need:
                        t.wait_ge(self.sem(f"p5_evv{jo % 2}"), need)
                    for e_i in range(NE):
                        mi = t.matmul(pj, wsrc[:, jo * NE * DH + e_i * DH: jo * NE * DH + (e_i + 1) * DH],
                                      x2t[p][:, e_i * 512:(e_i + 1) * 512],
                                      start=(e_i == 0), stop=(e_i == NE - 1))
                    smm, vmm = self.bump("p5_mm")
                    mi.then_inc(smm, 1)
                    v.wait_ge(smm, vmm)
                    if kind < 2:
                        k = jo % 4
                        if ev_st[k]:
                            v.wait_ge(self.sem(f"p5_st{k}"), ev_st[k])
                        sev, vev = self.bump(f"p5_evv{jo % 2}")
                        v.tensor_scalar(ev[k], pj, bias[:, jo:jo + 1], None, ALU.add).then_inc(sev, 1)
                        s.wait_ge(sev, vev)
                        if ev_st[k]:
                            s.wait_ge(self.sem(f"p5_st{k}"), ev_st[k])
                        sst, vst = self.bump(f"p5_st{k}", 16)
                        s.dma_start(dest[jo * 128:(jo + 1) * 128, tb * 512:(tb + 1) * 512],
                                    ev[k]).then_inc(sst, 16)
                        ev_st[k] = vst
                        self.emit_build_ops(3)
                    else:
                        if getattr(self, "_p5_vt_free", 0):
                            v.wait_ge(self.sem("p5_mm"), self._p5_vt_free)
                        sev, vev = self.bump(f"p5_evv{jo % 2}")
                        v.tensor_scalar(vt_sb, pj, bias[:, jo:jo + 1], None, ALU.add).then_inc(sev, 1)
                        t.wait_ge(sev, vev)
                        for ts in range(4):
                            ptr = tr_ps[ts % 2]
                            need3 = self.cur("p5_tre") - 1 if ts >= 2 else 0
                            if need3 > 0:
                                t.wait_ge(self.sem("p5_tre"), need3)
                            mi = t.transpose(ptr, vt_sb[:, ts * 128:(ts + 1) * 128], self.id_sb)
                            smm, vmm = self.bump("p5_mm")
                            mi.then_inc(smm, 1)
                            v.wait_ge(smm, vmm)
                            if tv_st[ts % 2]:
                                v.wait_ge(self.sem(f"p5_sv{ts % 2}"), tv_st[ts % 2])
                            ste, vte = self.bump("p5_tre")
                            v.tensor_copy(tv_sb[ts % 2], ptr).then_inc(ste, 1)
                            s.wait_ge(ste, vte)
                            if tv_st[ts % 2]:
                                s.wait_ge(self.sem(f"p5_sv{ts % 2}"), tv_st[ts % 2])
                            sst, vst = self.bump(f"p5_sv{ts % 2}", 16)
                            s.dma_start(self.V1[jo, tb * 512 + ts * 128: tb * 512 + (ts + 1) * 128, :],
                                        tv_sb[ts % 2]).then_inc(sst, 16)
                            tv_st[ts % 2] = vst
                        self._p5_vt_free = self.cur("p5_mm")
            x2_free[p] = self.cur("p5_mm")
            self.emit_tr(2)
        self.p5_done = self.phase_done("p5")
        self.last_done = self.p5_done
        st.close()

    def p6(self):
        self.dma_entry_fence()
        c = self.cfg
        t, v, sc, g, s = (self.q["tensor"], self.q["vector"], self.q["scalar"],
                          self.q["gpsimd"], self.q["sync"])
        NE, NT, TB, NG = c.NE, c.NT, c.TB, c.NG
        PER = c.NH // NG
        st = ExitStack()
        kt_sb = self.sbt(st, "p6_kt", [128, c.T], BF16)
        vv_sb = self.sbt(st, "p6_vv", [128, NT * DH], BF16)
        qt_sb = self.sbt(st, "p6_qt", [128, c.T], BF16)
        sq_sb = self.sbt(st, "p6_sq", [128, c.T], BF16)
        p_sb = [self.sbt(st, f"p6_p{i}", [128, NT * 512], BF16) for i in range(2)]
        o_sb = [self.sbt(st, f"p6_o{i}", [128, 512], BF16) for i in range(2)]
        r_sb = [self.sbt(st, f"p6_r{i}", [1, 512], F32) for i in range(2)]
        lnt_sb = [self.sbt(st, f"p6_lnt{i}", [1, 512], F32) for i in range(2)]
        br_sb = self.sbt(st, "p6_br", [128, 512], F32)
        mx_sb = self.sbt(st, "p6_mx", [1, 16], F32)
        biasb = self.sbt(st, "p6_bias", [128, 1], F32)

        self.mask_sb = self.sbt(st, "mask_sb", [128, 4 * 512], BF16)
        if self.cur("ld_mask"):
            g.wait_ge(self.sem("ld_mask"), self.cur("ld_mask"))
        smk, vmk = self.bump("ld_mask", 16)
        g.dma_start(self.mask_sb, self.c_masks).then_inc(smk, 16)
        sp_pair = [self.pst(st, f"p6_s{i}", [128, 1024]) for i in range(2)]
        op_ps = [self.pst(st, f"p6_op{i}", [128, 512]) for i in range(2)]
        aux_ps = self.pst(st, "p6_aux", [128, 512])
        sum_ps = [aux_ps[0:1, :], aux_ps[32:33, :]]
        nrm_ps = aux_ps[64:65, :]
        br_ps = self.pst(st, "p6_brp", [128, 512])
        nrm2_ps = br_ps[0:1, :]

        nchunk = c.T // 512
        o_st = [0, 0]
        rv_used = [0, 0]
        nrm_rd = [0, 0]
        pvmm = [0, 0]
        self.fence(s, self.p5_done)
        self.fence(g, self.p5_done)
        v.wait_ge(smk, vmk)
        for gg in range(NG):
            mmg_prev = self.cur("p6_mm")
            acg_prev = self.cur("p6_ac")
            g.wait_ge(self.sem("p6_mm"), mmg_prev)
            skh, vkh = self.bump("p6_ldk", 32)
            g.dma_start(kt_sb, self.KT1[gg * 128:(gg + 1) * 128, :]).then_inc(skh, 16)
            g.dma_start(vv_sb, self.V1[gg].rearrange("(a p) c -> p a c", p=128)).then_inc(skh, 16)
            sc.wait_ge(skh, vkh)
            if gg > 0:
                sc.wait_ge(self.sem("p6_mm"), mmg_prev)
            ska, vka = self.bump("p6_ac")
            sc.activation(sq_sb, kt_sb, AF.Square).then_inc(ska, 1)
            t.wait_ge(self.sem("p6_ac"), acg_prev)
            t.wait_ge(self.sem("p6_vc"), self.cur("p6_vc"))
            for ci in range(nchunk):
                t.wait_ge(ska, vka)
                bk = ci % 2
                nb = nrm_ps if bk == 0 else nrm2_ps
                if nrm_rd[bk]:
                    t.wait_ge(self.sem("p6_nrx"), nrm_rd[bk])
                mi = t.matmul(nb, self.ones_sb, sq_sb[:, ci * 512:(ci + 1) * 512],
                              start=True, stop=True)
                smm, vmm = self.bump("p6_mm")
                mi.then_inc(smm, 1)
                v.wait_ge(smm, vmm)
                sx, vx = self.bump("p6_nrx")
                v.reduce_max(mx_sb[:, 8 + ci:8 + ci + 1], nb, axis=AX).then_inc(sx, 1)
                nrm_rd[bk] = vx
            v.wait_ge(self.sem("p6_nrx"), self.cur("p6_nrx"))
            sv2, vv2 = self.bump("p6_vc")
            v.reduce_max(mx_sb[:, 15:16], mx_sb[:, 8:8 + nchunk], axis=AX).then_inc(sv2, 1)

            for hh in range(PER):
                h = gg * PER + hh
                mm_prev = self.cur("p6_mm")
                ac_prev = self.cur("p6_ac")
                g.wait_ge(self.sem("p6_mm"), mm_prev)
                if self.cur("p6_ldq"):
                    g.wait_ge(self.sem("p6_ldq"), self.cur("p6_ldq"))
                sqh, vqh = self.bump("p6_ldq", 16)
                g.dma_start(qt_sb, self.QT1[h * 128:(h + 1) * 128, :]).then_inc(sqh, 16)
                sc.wait_ge(sqh, vqh)
                sc.wait_ge(self.sem("p6_mm"), mm_prev)
                sa, va = self.bump("p6_ac")
                sc.activation(sq_sb, qt_sb, AF.Square).then_inc(sa, 1)
                t.wait_ge(self.sem("p6_vc"), self.cur("p6_vc"))
                for ci in range(nchunk):
                    t.wait_ge(sa, va)
                    bk = ci % 2
                    nb = nrm_ps if bk == 0 else nrm2_ps
                    if nrm_rd[bk]:
                        t.wait_ge(self.sem("p6_nrx"), nrm_rd[bk])
                    mi = t.matmul(nb, self.ones_sb, sq_sb[:, ci * 512:(ci + 1) * 512],
                                  start=True, stop=True)
                    smm, vmm = self.bump("p6_mm")
                    mi.then_inc(smm, 1)
                    v.wait_ge(smm, vmm)
                    sx, vx = self.bump("p6_nrx")
                    v.reduce_max(mx_sb[:, ci:ci + 1], nb, axis=AX).then_inc(sx, 1)
                    nrm_rd[bk] = vx
                v.wait_ge(self.sem("p6_nrx"), self.cur("p6_nrx"))
                sv1, vv1 = self.bump("p6_vc")
                v.reduce_max(mx_sb[:, 14:15], mx_sb[:, 0:nchunk], axis=AX).then_inc(sv1, 1)
                v.wait_ge(sv1, vv1)
                sv3, vv3 = self.bump("p6_vc")
                v.tensor_tensor(mx_sb[:, 13:14], mx_sb[:, 14:15], mx_sb[:, 15:16],
                                ALU.mult).then_inc(sv3, 1)
                sc.wait_ge(sv3, vv3)
                sa4, va4 = self.bump("p6_ac")
                sc.activation(mx_sb[:, 12:13], mx_sb[:, 13:14], AF.Sqrt).then_inc(sa4, 1)
                sc.wait_ge(sa4, va4)
                sa5, va5 = self.bump("p6_ac")
                sc.mul(mx_sb[:, 11:12], mx_sb[:, 12:13], -1.01 * ISCALE).then_inc(sa5, 1)
                t.wait_ge(sa5, va5)
                mi = t.matmul(br_ps[:, 0:1], self.onesr_sb, mx_sb[:, 11:12], start=True, stop=True)
                smm, vmm = self.bump("p6_mm")
                mi.then_inc(smm, 1)
                v.wait_ge(smm, vmm)
                if gg + hh > 0:
                    v.wait_ge(self.sem("p6_ac"), ac_prev)
                sbb, vbb = self.bump("p6_bv")
                v.tensor_copy(biasb, br_ps[:, 0:1]).then_inc(sbb, 1)
                sc.wait_ge(sbb, vbb)

                pend = None
                prev = None
                br_ev = [0]

                def fin6(qb_, rv_, h=h):
                    pqb_ = qb_ % 2
                    t.wait_ge(self.sem("p6_ac"), rv_)
                    if br_ev[0]:
                        t.wait_ge(self.sem("p6_vc"), br_ev[0])
                    mi_ = t.matmul(br_ps, self.onesr_sb, r_sb[pqb_], start=True, stop=True)
                    smm_, vmm_ = self.bump("p6_mm")
                    mi_.then_inc(smm_, 1)
                    v.wait_ge(smm_, vmm_)
                    for pp_ in range(2):
                        if self.cur(f"p6_pv{pp_}"):
                            v.wait_ge(self.sem(f"p6_pv{pp_}"), self.cur(f"p6_pv{pp_}"))
                    se_, ve_ = self.bump("p6_vc")
                    v.tensor_copy(br_sb, br_ps).then_inc(se_, 1)
                    br_ev[0] = ve_
                    v.wait_ge(se_, ve_)
                    if o_st[pqb_]:
                        v.wait_ge(self.sem(f"p6_st{pqb_}"), o_st[pqb_])
                    so_, vo_ = self.bump(f"p6_pv{pqb_}")
                    v.tensor_tensor(o_sb[pqb_], op_ps[pqb_], br_sb, ALU.mult).then_inc(so_, 1)
                    s.wait_ge(so_, vo_)
                    if o_st[pqb_]:
                        s.wait_ge(self.sem(f"p6_st{pqb_}"), o_st[pqb_])
                    sst_, vst_ = self.bump(f"p6_st{pqb_}", 16)
                    s.dma_start(self.OT0[h * 128:(h + 1) * 128, qb_ * 512:(qb_ + 1) * 512],
                                o_sb[pqb_]).then_inc(sst_, 16)
                    o_st[pqb_] = vst_

                def pv_tile6(qb_, eb_, mb_, ntv_, tt_, first, last):
                    pq_ = qb_ % 2
                    if tt_ >= ntv_ - 4:
                        di_ = (tt_ - (ntv_ - 4)) // 2
                        t.wait_ge(self.sem("p6_msk"), mb_ + di_ + 1)
                    if first:
                        if self.cur(f"p6_pv{pq_}"):
                            t.wait_ge(self.sem(f"p6_pv{pq_}"), self.cur(f"p6_pv{pq_}"))
                        if rv_used[pq_]:
                            t.wait_ge(self.sem("p6_ac"), rv_used[pq_])
                    t.matmul(op_ps[pq_], vv_sb[:, tt_ * DH:(tt_ + 1) * DH],
                             p_sb[pq_][:, tt_ * 512:(tt_ + 1) * 512],
                             start=first, stop=last, skip_group_check=True)
                    mi_ = t.matmul(sum_ps[pq_], self.ones_sb,
                                   p_sb[pq_][:, tt_ * 512:(tt_ + 1) * 512],
                                   start=first, stop=last, skip_group_check=True)
                    if last:
                        smm_, vmm_ = self.bump("p6_mm")
                        mi_.then_inc(smm_, 1)
                        return vmm_
                    return None

                def b_fin6(qb_, vmm_):
                    nonlocal pend
                    pq_ = qb_ % 2
                    sc.wait_ge(self.sem("p6_mm"), vmm_)
                    sr, vr = self.bump("p6_ac")
                    sc.activation(lnt_sb[pq_], sum_ps[pq_], AF.Ln).then_inc(sr, 1)
                    sc.wait_ge(sr, vr)
                    sr2, vr2 = self.bump("p6_ac")
                    sc.activation(r_sb[pq_], lnt_sb[pq_], AF.Exp, scale=-1.0).then_inc(sr2, 1)
                    rv_used[pq_] = vr2
                    pvmm[pq_] = vmm_
                    if pend is not None:
                        fin6(*pend)
                    pend = (qb_, vr2)

                for qb in range(TB):
                    pqb = qb % 2
                    nt_valid = qb * 4 + 4
                    npr = nt_valid // 2
                    if pvmm[pqb]:
                        sc.wait_ge(self.sem("p6_mm"), pvmm[pqb])
                    exp_base = self.cur("p6_ac")
                    msk_base = self.cur("p6_msk")
                    bvmm = None
                    for pr in range(npr):
                        tt0 = 2 * pr
                        pj = sp_pair[pr % 2]
                        if pr >= 2:
                            t.wait_ge(self.sem("p6_ac"), exp_base + pr - 1)
                        elif pr == 0 and prev is not None:
                            t.wait_ge(self.sem("p6_ac"), prev[1] + prev[3] // 2)
                        mi = t.matmul(pj[:, 0:512], kt_sb[:, tt0 * 128:(tt0 + 1) * 128],
                                      qt_sb[:, qb * 512:(qb + 1) * 512], start=True, stop=True)
                        mi = t.matmul(pj[:, 512:1024], kt_sb[:, (tt0 + 1) * 128:(tt0 + 2) * 128],
                                      qt_sb[:, qb * 512:(qb + 1) * 512], start=True, stop=True)
                        smm, vmm = self.bump("p6_mm")
                        mi.then_inc(smm, 1)
                        sc.wait_ge(smm, vmm)
                        sae, _ = self.bump("p6_ac")
                        sc.activation(p_sb[pqb][:, tt0 * 512:(tt0 + 2) * 512], pj, AF.Exp,
                                      bias=biasb, scale=ISCALE).then_inc(sae, 1)
                        if tt0 >= nt_valid - 4:
                            di = tt0 - (nt_valid - 4)
                            v.wait_ge(self.sem("p6_ac"), exp_base + pr + 1)
                            smk, _ = self.bump("p6_msk")
                            v.tensor_tensor(p_sb[pqb][:, tt0 * 512:(tt0 + 2) * 512],
                                            p_sb[pqb][:, tt0 * 512:(tt0 + 2) * 512],
                                            self.mask_sb[:, di * 512:(di + 2) * 512],
                                            ALU.mult).then_inc(smk, 1)
                        if prev is not None and tt0 < prev[3]:
                            for d_ in (0, 1):
                                ttp = tt0 + d_
                                r_ = pv_tile6(prev[0], prev[1], prev[2], prev[3],
                                              ttp, ttp == 0, ttp == prev[3] - 1)
                                if r_ is not None:
                                    bvmm = r_
                        self.emit_build_ops(2)
                    if prev is not None:
                        b_fin6(prev[0], bvmm)
                    prev = (qb, exp_base, msk_base, nt_valid)
                    self.emit_tr(1)
                qb_, eb_, mb_, ntv_ = prev
                bvmm = None
                for tt in range(ntv_):
                    if tt % 2 == 0 and tt < ntv_ - 4:
                        t.wait_ge(self.sem("p6_ac"), eb_ + tt // 2 + 1)
                    r_ = pv_tile6(qb_, eb_, mb_, ntv_, tt, tt == 0, tt == ntv_ - 1)
                    if r_ is not None:
                        bvmm = r_
                b_fin6(qb_, bvmm)
                if pend is not None:
                    fin6(*pend)
                    pend = None
                prev = None
        self.p6_done = self.phase_done("p6")
        self.last_done = self.p6_done
        st.close()

    def p6b(self):
        self.dma_entry_fence()
        c = self.cfg
        t, v, g, s = self.q["tensor"], self.q["vector"], self.q["gpsimd"], self.q["sync"]
        NE, TB = c.NE, c.TB
        st = ExitStack()
        ot = [self.sbt(st, f"p6b_ot{i}", [128, NE * 512], BF16) for i in range(2)]
        x2t = [self.sbt(st, f"p6b_x2{i}", [128, NE * 512], BF16) for i in range(2)]
        sum_ps = self.pst(st, "p6b_sum", [1, 512])
        sq_ps = self.pst(st, "p6b_sq", [1, 512])
        mb_ps = self.pst(st, "p6b_mbp", [128, 512])
        rb_ps = self.pst(st, "p6b_rbp", [128, 512])
        lnr = self.make_ln_res(st, "p6b", sum_ps, sq_ps, mb_ps, rb_ps, BF16)

        self.fence(s, self.p6_done)
        self.fence(g, self.p6_done)
        buf_free = [0, 0]
        for tb in range(TB):
            p = tb % 2
            if buf_free[p]:
                z0, z1 = buf_free[p]
                g.wait_ge(self.sem("p6b_zadd0"), z0)
                g.wait_ge(self.sem("p6b_zadd1"), z1)
            if self.cur(f"p6b_ldo{p}"):
                g.wait_ge(self.sem(f"p6b_ldo{p}"), self.cur(f"p6b_ldo{p}"))
            slo, vlo = self.bump(f"p6b_ldo{p}", 16 * 2 * NE)
            for e_i in range(NE):
                g.dma_start(ot[p][:, e_i * 512:(e_i + 1) * 512],
                            self.OT0[e_i * 128:(e_i + 1) * 128,
                                     tb * 512:(tb + 1) * 512]).then_inc(slo, 16)
                g.dma_start(x2t[p][:, e_i * 512:(e_i + 1) * 512],
                            self.X2[e_i * 128:(e_i + 1) * 128,
                                    tb * 512:(tb + 1) * 512]).then_inc(slo, 16)
            v.wait_ge(slo, vlo)
            for j in range(NE):
                self.ln_z("p6b", lnr, j, ot[p][:, j * 512:(j + 1) * 512], [],
                          x2t[p][:, j * 512:(j + 1) * 512])
                self.emit_build_ops(2)
            self.ln_finish("p6b", lnr, tb, self.X3)
            self.emit_tr(2)
            buf_free[p] = (self.cur("p6b_zadd0"), self.cur("p6b_zadd1"))
        self.p6b_done = self.phase_done("p6b")
        self.last_done = self.p6b_done
        st.close()

    def p7(self):
        self.emit_build(len(self.bw_jobs))
        self.p7_done = [("bw_sta", self.cur("bw_sta"))]
        self.last_done = self.p7_done

    def p7b(self):
        self.emit_tr(len(self.tr_jobs), margin=0)
        self._tr_store()
        self.p7b_done = [("tr_st0", self.tr_st[0]), ("tr_st1", self.tr_st[1])]
        self.last_done = self.p7b_done

    def p8(self):
        self.dma_entry_fence(exclude=("tr_", "bw_"))
        g0 = self.q["gpsimd"]
        g0.wait_ge(self.sem("bw_sta"), self.bw_sta)
        g0.wait_ge(self.sem("tr_st0"), self.tr_mark_uw[0])
        g0.wait_ge(self.sem("tr_st1"), self.tr_mark_uw[1])
        c = self.cfg
        t, v, sc, g, s = (self.q["tensor"], self.q["vector"], self.q["scalar"],
                          self.q["gpsimd"], self.q["sync"])
        NE, TB, NI = c.NE, c.TB, c.NI
        st = ExitStack()
        x3t = [self.sbt(st, f"p8_x3{i}", [128, NE * 512], BF16) for i in range(2)]
        gst = [self.sbt(st, f"p8_gs{i}", [128, NE * 128], BF16) for i in range(2)]
        ust = [self.sbt(st, f"p8_us{i}", [128, NE * 128], BF16) for i in range(2)]
        dst_b = [self.sbt(st, f"p8_ds{i}", [128, NI * 256], BF16) for i in range(2)]
        h_sb = self.sbt(st, "p8_h", [128, NI * 512], BF16)
        sil_sb = self.sbt(st, "p8_sil", [128, 512], BF16)

        g_ps = self.pst(st, "p8_g", [128, 512])
        u_ps = self.pst(st, "p8_u", [128, 512])
        d_ps = [self.pst(st, f"p8_d{i}", [128, 512]) for i in range(4)]
        sum_ps = self.pst(st, "p8_sum", [1, 512])
        sq_ps = self.pst(st, "p8_sq", [1, 512])
        lnr = self.make_ln_res(st, "p8", sum_ps, sq_ps, g_ps, u_ps, F32)

        self.fence(s, self.p6b_done)
        self.fence(s, self.p7b_done)
        self.fence(g, self.p6b_done)
        self.fence(g, self.p7b_done)

        x3_free = [0, 0]
        gs_free = [0, 0]
        ds_free = [0, 0]
        for tb in range(TB):
            p = tb % 2
            if x3_free[p]:
                g.wait_ge(self.sem("p8_mm"), x3_free[p][0])
                for bi in range(4):
                    if x3_free[p][1 + bi]:
                        g.wait_ge(self.sem(f"p8_zadd{bi}"), x3_free[p][1 + bi])
            if self.cur(f"p8_ldo{p}"):
                g.wait_ge(self.sem(f"p8_ldo{p}"), self.cur(f"p8_ldo{p}"))
            slx, vlx = self.bump(f"p8_ldo{p}", 16 * NE)
            for e_i in range(NE):
                g.dma_start(x3t[p][:, e_i * 512:(e_i + 1) * 512],
                            self.X3[e_i * 128:(e_i + 1) * 128,
                                    tb * 512:(tb + 1) * 512]).then_inc(slx, 16)
            t.wait_ge(slx, vlx)
            hmm_gate = self.cur("p8_mm") if tb > 0 else 0
            for it in range(NI):
                pw = it % 2
                if gs_free[pw]:
                    g.wait_ge(self.sem("p8_mm"), gs_free[pw])
                slw, vlw = self.bump(f"p8_ldw{pw}", 32)
                g.dma_start(gst[pw],
                            self.GWT[:, it * 128:(it + 1) * 128]
                                .rearrange("(a p) c -> p a c", p=128)).then_inc(slw, 16)
                g.dma_start(ust[pw],
                            self.UWT[:, it * 128:(it + 1) * 128]
                                .rearrange("(a p) c -> p a c", p=128)).then_inc(slw, 16)
                t.wait_ge(slw, vlw)
                need = self.cur("p8_sil")
                if need:
                    t.wait_ge(self.sem("p8_sil"), need)
                needh = self.cur("p8_hw")
                if needh:
                    t.wait_ge(self.sem("p8_hw"), needh)
                if it == 0 and lnr.get("free_wait"):
                    t.wait_ge(*lnr["free_wait"])
                for e_i in range(NE):
                    mi = t.matmul(g_ps, gst[pw][:, e_i * 128:(e_i + 1) * 128],
                                  x3t[p][:, e_i * 512:(e_i + 1) * 512],
                                  start=(e_i == 0), stop=(e_i == NE - 1), skip_group_check=True)
                for e_i in range(NE):
                    mi2 = t.matmul(u_ps, ust[pw][:, e_i * 128:(e_i + 1) * 128],
                                   x3t[p][:, e_i * 512:(e_i + 1) * 512],
                                   start=(e_i == 0), stop=(e_i == NE - 1), skip_group_check=True)
                smm, vmm = self.bump("p8_mm")
                mi2.then_inc(smm, 1)
                gs_free[pw] = vmm
                sc.wait_ge(smm, vmm)
                if self.cur("p8_hw"):
                    sc.wait_ge(self.sem("p8_hw"), self.cur("p8_hw"))
                ssl, vsl = self.bump("p8_sil")
                sc.activation(sil_sb, g_ps, AF.Silu).then_inc(ssl, 1)
                v.wait_ge(ssl, vsl)
                if tb > 0 and it == 0:
                    v.wait_ge(self.sem("p8_mm"), hmm_gate)
                shw, _ = self.bump("p8_hw")
                v.tensor_tensor(h_sb[:, it * 512:(it + 1) * 512], sil_sb, u_ps,
                                ALU.mult).then_inc(shw, 1)
            hw_all = self.cur("p8_hw")
            for og in range(NE // 2):
                pw = og % 2
                if ds_free[pw]:
                    g.wait_ge(self.sem("p8_mm"), ds_free[pw])
                if self.cur(f"p8_ldd{pw}"):
                    g.wait_ge(self.sem(f"p8_ldd{pw}"), self.cur(f"p8_ldd{pw}"))
                if tb == 0 and og < 2:
                    g.wait_ge(self.sem("tr_st0"), self.tr_mark_all[0])
                    g.wait_ge(self.sem("tr_st1"), self.tr_mark_all[1])
                slw, vlw = self.bump(f"p8_ldd{pw}", 16)
                g.dma_start(dst_b[pw],
                            self.DWT[:, og * 256:(og + 1) * 256]
                                .rearrange("(a p) c -> p a c", p=128)).then_inc(slw, 16)
                t.wait_ge(slw, vlw)
                if og == 0:
                    t.wait_ge(self.sem("p8_hw"), hw_all)
                for oi in range(2):
                    j = og * 2 + oi
                    bk = j % 4
                    need = self.cur(f"p8_zadd{bk}")
                    if need:
                        t.wait_ge(self.sem(f"p8_zadd{bk}"), need)
                for it in range(NI):
                    for oi in range(2):
                        mi = t.matmul(d_ps[(og * 2 + oi) % 4],
                                      dst_b[pw][:, it * 256 + oi * 128: it * 256 + (oi + 1) * 128],
                                      h_sb[:, it * 512:(it + 1) * 512],
                                      start=(it == 0), stop=(it == NI - 1), skip_group_check=True)
                smm, vmm = self.bump("p8_mm")
                mi.then_inc(smm, 1)
                ds_free[pw] = vmm
                for oi in range(2):
                    j = og * 2 + oi
                    self.ln_z("p8", lnr, j, d_ps[j % 4], [(smm, vmm)],
                              x3t[p][:, j * 512:(j + 1) * 512], bank=j % 4)
            self.ln_finish("p8", lnr, tb, self.OUT_T)
            x3_free[p] = (self.cur("p8_mm"), self.cur("p8_zadd0"), self.cur("p8_zadd1"),
                          self.cur("p8_zadd2"), self.cur("p8_zadd3"))
        self.p8_done = self.phase_done("p8")
        self.last_done = self.p8_done
        st.close()

    def finalize(self):
        s = self.q["sync"]
        self.fence(s, self.last_done)
        with self.nc.Block() as block:
            block.tensor(lambda e: self.q["tensor"].replay(e))
            block.vector(lambda e: self.q["vector"].replay(e))
            block.scalar(lambda e: self.q["scalar"].replay(e))
            block.gpsimd(lambda e: self.q["gpsimd"].replay(e))
            block.sync(lambda e: self.q["sync"].replay(e))
        return self.nc


def build(cfg: Cfg, debug_outputs=(), phases=None):
    b = Builder(cfg, debug_outputs)
    b.declare_io()
    b.declare_consts()
    allp = ["p1", "p2", "p3", "p4", "p5", "p6", "p6b", "p7", "p7b", "p8"]
    if phases is None:
        phases = allp
    b.build_init()
    for ph in phases:
        getattr(b, ph)()
    if b.tr_stack is not None:
        b.tr_stack.close()
        b.tr_stack = None
    b.bw_stack.close()
    return b.finalize(), b


def make_core_inputs(params, b):
    C = np.ascontiguousarray
    return {
        "x": params["x"],
        "wqT": C(params["wq"].T),
        "wdkvT": C(params["wdkv"].T),
        "wupkT": C(params["wupk"].transpose(0, 2, 1)),
        "wupvT": C(params["wupv"].transpose(0, 2, 1)),
        "woT": C(params["wo"].T),
        "wffT": C(params["wff"].T),
        "gatewT": C(params["gatew"].T),
        "awT": C(params["aw"].reshape(128, -1).T),
        "bwT": C(params["bw"].transpose(0, 2, 1).reshape(128, -1)),
        "wq1": params["wq1"],
        "bq1": params["bq1"],
        "wk1": params["wk1"],
        "bk1": params["bk1"],
        "wv1": params["wv1"],
        "bv1": params["bv1"],
        "lut": params["lut"].reshape(1, 16),
        "gidx": params["gidx"],
        "uidx": params["uidx"],
        "didx": params["didx"],
    }


def kernel(L0_attn_x, L0_attn_W_q, L0_attn_W_down_kv, L0_attn_W_up_k, L0_attn_W_up_v,
           L0_attn_W_o, L0_ffn_weight, L0_ffn_gate_w, L0_ffn_A_w, L0_ffn_B_w,
           L1_attn_Wq, L1_attn_bq, L1_attn_Wk, L1_attn_bk, L1_attn_Wv, L1_attn_bv,
           L1_ffn_lut, L1_ffn_gate_idx, L1_ffn_up_idx, L1_ffn_down_idx):
    x = np.asarray(L0_attn_x)
    B = x.shape[0]
    cfg = Cfg(T=x.shape[1], E=x.shape[2])
    nc, _ = build(cfg)
    shared = {
        "wq": np.asarray(L0_attn_W_q), "wdkv": np.asarray(L0_attn_W_down_kv),
        "wupk": np.asarray(L0_attn_W_up_k), "wupv": np.asarray(L0_attn_W_up_v),
        "wo": np.asarray(L0_attn_W_o), "wff": np.asarray(L0_ffn_weight),
        "gatew": np.asarray(L0_ffn_gate_w), "aw": np.asarray(L0_ffn_A_w),
        "bw": np.asarray(L0_ffn_B_w),
        "wq1": np.asarray(L1_attn_Wq), "bq1": np.asarray(L1_attn_bq),
        "wk1": np.asarray(L1_attn_Wk), "bk1": np.asarray(L1_attn_bk),
        "wv1": np.asarray(L1_attn_Wv), "bv1": np.asarray(L1_attn_bv),
        "lut": np.asarray(L1_ffn_lut), "gidx": np.asarray(L1_ffn_gate_idx),
        "uidx": np.asarray(L1_ffn_up_idx), "didx": np.asarray(L1_ffn_down_idx),
    }
    base = make_core_inputs(dict(shared, x=x[0]), None)
    in_maps = []
    for bi in range(B):
        m = dict(base)
        m["x"] = np.ascontiguousarray(x[bi])
        in_maps.append(m)
    res = run_bass_kernel_spmd(nc, in_maps, core_ids=list(range(8)))
    outs = [np.ascontiguousarray(np.asarray(r["OUT_T"]).T) for r in res.results]
    return np.stack(outs, 0)
